# version 05
# baseline (speedup 1.0000x reference)
import numpy as np
from contextlib import ExitStack

import concourse.bass as bass
import concourse.mybir as mybir
from concourse.bass_utils import run_bass_kernel_spmd

F32 = mybir.dt.float32
F32R = mybir.dt.float32r
BF16 = mybir.dt.bfloat16
AF = mybir.ActivationFunctionType
ALU = mybir.AluOpType
AX = mybir.AxisListType

D = 1024
S = 2048
DCH = 8
NP = 8
HS = 64
TS = 4
NE = 8
FF = 2048
FCH = 16
NW = 2
WTS = 2
NQT = 16
NB = 4
EPS = 1e-5

ENGINES = ("tensor", "scalar", "vector", "gpsimd", "sync")


class Prog:

    def __init__(self, nc):
        self.nc = nc
        self.streams = {e: [] for e in ENGINES}
        self.sems = {}
        self.counts = {}
        self.waited = {e: {} for e in ENGINES}
        self.ctx = None

    def sem(self, name):
        h = self.ctx.enter_context(self.nc.semaphore(name))
        self.sems[name] = h
        self.counts[name] = 0
        return name

    def wait(self, eng, mark):
        if mark is None:
            return
        sem, val = mark
        if val <= 0:
            return
        if self.waited[eng].get(sem, 0) >= val:
            return
        self.waited[eng][sem] = val
        h = self.sems[sem]
        self.streams[eng].append(("w", h, val))

    def op(self, eng, fn, inc=None):
        if inc is not None:
            sem, amt = inc
            self.counts[sem] += amt
            self.streams[eng].append(("o", fn, self.sems[sem], amt))
            return (sem, self.counts[sem])
        self.streams[eng].append(("o", fn, None, 0))
        return None

    def cur(self, sem):
        return (sem, self.counts[sem])


class Res:

    def __init__(self):
        self.ready = None
        self.users = []

    def acquire(self, prog, eng):
        prog.wait(eng, self.ready)
        for m in self.users:
            prog.wait(eng, m)

    def produce(self, mark):
        self.ready = mark
        self.users = []

    def consume_wait(self, prog, eng):
        prog.wait(eng, self.ready)

    def used(self, mark):
        if mark is not None:
            self.users.append(mark)


TAPB = 0


def build_program(n_batch=NB, taps=(), phases=("attn", "ffn")):
    nc = bass.Bass("TRN2", target_bir_lowering=False)
    p = Prog(nc)

    xT = nc.dram_tensor("xT", [n_batch, DCH, 128, S], F32, kind="ExternalInput").ap()
    wqkv = nc.dram_tensor("wqkv", [NP, 3, DCH, 128, 128], F32, kind="ExternalInput").ap()
    wm = nc.dram_tensor("wm", [DCH, DCH, 128, 128], F32, kind="ExternalInput").ap()
    ffn = nc.dram_tensor("ffn", [NE, 384, 128, 128], F32, kind="ExternalInput").ap()
    bias3 = nc.dram_tensor("bias3", [3, D], F32, kind="ExternalInput").ap()
    bmv = nc.dram_tensor("bmv", [1, D], F32, kind="ExternalInput").ap()
    rwt = nc.dram_tensor("rwt", [DCH, 128, 8], F32, kind="ExternalInput").ap()
    rbv = nc.dram_tensor("rbv", [128, 8], F32, kind="ExternalInput").ap()
    tempv = nc.dram_tensor("tempv", [128, 1], F32, kind="ExternalInput").ap()
    identv = nc.dram_tensor("identv", [128, 128], F32, kind="ExternalInput").ap()
    maskv = nc.dram_tensor("maskv", [128, 128], F32, kind="ExternalInput").ap()
    y = nc.dram_tensor("y", [n_batch, DCH, 128, S], F32, kind="ExternalOutput").ap()
    tap_dram = {}
    for tname, shape in taps:
        tap_dram[tname] = nc.dram_tensor("tap_" + tname, list(shape), F32,
                                         kind="ExternalOutput").ap()

    SBUF_BASE = 16512
    cursor = [SBUF_BASE]

    def alloc(name, shape, dtype, at=None):
        nbytes = {F32: 4, F32R: 4, BF16: 2}[dtype] * shape[-1]
        if at is None:
            at = cursor[0]
            cursor[0] = at + ((nbytes + 31) // 32) * 32
        t = nc.alloc_sbuf_tensor_at(name, list(shape), dtype, offset=at)
        return t, at

    x_sb, x_off = alloc("x", (128, DCH * S), F32)
    x_r, _ = alloc("x_r", (128, DCH * S), F32R, at=x_off)
    lnT, _ = alloc("lnT", (128, DCH * S), BF16)
    attnT, _ = alloc("attnT", (128, DCH * S), BF16)

    arena_base = cursor[0]
    qkv_off = [arena_base + i * (3 * S * 2) for i in range(2)]
    v_off = [arena_base + 2 * (3 * S * 2) + i * 4160 for i in range(2)]
    probs_off = [arena_base + 24576 + 8320 + i * (S * 2) for i in range(2)]
    asb_off = arena_base + 24576 + 8320 + 8192
    attn_arena_end = asb_off + S * 2
    h_off = arena_base
    psb_off = arena_base + FCH * 1024 * 2
    ffn_arena_end = psb_off + 2 * 1024 * 2
    cursor[0] = ((max(attn_arena_end, ffn_arena_end) + 31) // 32) * 32

    q_sb, k_sb, vt_sb, v_sb, probs_sb = [], [], [], [], []
    for i in range(2):
        t, _ = alloc(f"q{i}", (128, S), BF16, at=qkv_off[i]); q_sb.append(t)
        t, _ = alloc(f"k{i}", (128, S), BF16, at=qkv_off[i] + S * 2); k_sb.append(t)
        t, _ = alloc(f"vt{i}", (128, S), BF16, at=qkv_off[i] + 2 * S * 2); vt_sb.append(t)
        t, _ = alloc(f"v{i}", (128, 2 * 16 * 65), BF16, at=v_off[i]); v_sb.append(t)
        t, _ = alloc(f"probs{i}", (128, S), BF16, at=probs_off[i]); probs_sb.append(t)
    asb, _ = alloc("asb", (128, S), BF16, at=asb_off)
    h_sb, _ = alloc("h", (128, FCH * 1024), BF16, at=h_off)
    p_sb, _ = alloc("psb", (128, 2 * 1024), BF16, at=psb_off)

    RING = 32
    ring_sb, _ = alloc("ring", (128, RING * 128), BF16)
    rows, rows_off = alloc("rows", (65, S), F32)
    rows_r, _ = alloc("rows_r", (65, S), F32R, at=rows_off)
    scr, scr_off = alloc("scr", (128, S), F32)
    scr_r, _ = alloc("scr_r", (128, S), F32R, at=scr_off)
    scr2, scr2_off = alloc("scr2", (128, 512), F32)
    vrow, _ = alloc("vrow", (1, 512), F32, at=scr2_off)
    vrow2, _ = alloc("vrow2", (1, 512), F32, at=scr_off + 4096)
    b3_sb, _ = alloc("b3", (65, D), BF16)
    bm_sb, _ = alloc("bm", (1, D), BF16)
    row0p = []
    for i in range(2):
        t, _ = alloc(f"row0{i}", (1, 1024), BF16, at=asb_off + 2048 * i)
        row0p.append(t)
    wb2 = []
    negB, _ = alloc("negB", (128, 4), F32)
    la_sb, _ = alloc("la", (128, 16 * 8), F32)
    sink, _ = alloc("sink", (128, 64), F32)
    wbt_sb, _ = alloc("wbt", (128, 16 * 8), BF16)
    mcols, _ = alloc("mcols", (128, 32), F32)
    for i in range(2):
        t, _ = alloc(f"wb{i}", (128, 1024), BF16, at=arena_base + 36864 + 2048 * i)
        wb2.append(t)
    rw_sb, rw_off = alloc("rw", (128, 64), F32)
    rw_r, _ = alloc("rw_r", (128, 64), F32R, at=rw_off)
    ident, _ = alloc("ident", (128, 128), BF16)
    caus, _ = alloc("caus", (128, 128), BF16)
    ones3, _ = alloc("ones3", (65, 512), BF16)
    ones65, o65_off = alloc("ones65", (65, 128), F32)
    ones65_r, _ = alloc("ones65_r", (65, 128), F32R, at=o65_off)
    onecol, oc_off = alloc("onecol", (128, 1), F32)
    onecol_r, _ = alloc("onecol_r", (128, 1), F32R, at=oc_off)
    tiny, _ = alloc("tiny", (1, 16), F32)
    tiny2, _ = alloc("tiny2", (1, 24), F32)
    rbt_sb, _ = alloc("rbt", (128, 8), F32)
    cs_sb, _ = alloc("cs", (128, 8), F32)
    assert cursor[0] <= nc.sbuf_top, (cursor[0], nc.sbuf_top)

    pb = [nc.ctx.enter_context(nc.psum_tensor(f"pb{i}", [128, 512], F32))
          for i in range(6)]
    pT = nc.ctx.enter_context(nc.psum_tensor("pT", [128, 1024], BF16))
    pS = nc.ctx.enter_context(nc.psum_tensor("pS", [128, 512], F32))
    B = [Res() for _ in range(6)]
    BT = Res()
    BS = Res()

    p.ctx = nc.ctx
    s_t = p.sem("s_t")
    s_s = p.sem("s_s")
    s_v = p.sem("s_v")
    s_g = p.sem("s_g")
    s_w = p.sem("s_w")
    s_x = p.sem("s_x")
    s_y = p.sem("s_y")
    MARKS = {"tensor": s_t, "scalar": s_s, "vector": s_v, "gpsimd": s_g}

    def M(eng):
        return (MARKS[eng], 1)

    def mm(out, lhsT, rhs, start, stop, mark=False):
        return p.op("tensor",
                    lambda t, o=out, a=lhsT, b=rhs, st=start, sp=stop:
                    t.matmul(o, a, b, start=st, stop=sp),
                    inc=M("tensor") if mark else None)

    def tr(out, in_, mark=False):
        return p.op("tensor",
                    lambda t, o=out, a=in_: t.transpose(o, a, ident[:, :]),
                    inc=M("tensor") if mark else None)

    def sc(fn, mark=False):
        return p.op("scalar", fn, inc=M("scalar") if mark else None)

    def ve(fn, mark=False):
        return p.op("vector", fn, inc=M("vector") if mark else None)

    ring_pos = [0]
    ring_chunks = []
    ring_dmas = [0]
    ring_freed = [0]
    free_marks = {}

    def ring_load(src_ap, n_tiles):
        start = ring_pos[0]
        slots = []
        segs = []
        t0 = start
        rem = n_tiles
        src_i = 0
        while rem > 0:
            slot = t0 % RING
            seg = min(rem, RING - slot)
            segs.append((slot, src_i, seg))
            t0 += seg
            src_i += seg
            rem -= seg
        last_tile = start + n_tiles - 1
        need_freed_tile = last_tile - RING
        if need_freed_tile >= 0:
            for ci, (cs, ce, _) in enumerate(ring_chunks):
                if cs <= need_freed_tile < ce:
                    wantc = ci + 1
                    break
            else:
                wantc = 0
            for ci in range(ring_freed[0], wantc):
                p.wait("gpsimd", free_marks[ci])
            ring_freed[0] = max(ring_freed[0], wantc)
        for slot, si, seg in segs:
            dst = ring_sb[:, slot * 128:(slot + seg) * 128].rearrange(
                "p (n f) -> p n f", f=128)
            sap = src_ap[si:si + seg, :, :].rearrange("n p f -> p n f")
            p.op("gpsimd",
                 lambda g, d=dst, s=sap: g.dma_start(out=d, in_=s),
                 inc=(s_w, 16))
            ring_dmas[0] += 1
        ring_chunks.append((start, start + n_tiles, ring_dmas[0]))
        ring_pos[0] = start + n_tiles
        wait_mark = (s_w, 16 * ring_dmas[0])
        chunk_idx = len(ring_chunks) - 1
        for i in range(n_tiles):
            slots.append(ring_sb[:, ((start + i) % RING) * 128:((start + i) % RING) * 128 + 128])
        return slots, wait_mark, chunk_idx

    def ring_free(chunk_idx, mark):
        free_marks[chunk_idx] = mark

    def gdma(dst, src):
        p.op("gpsimd", lambda g, d=dst, s=src: g.dma_start(out=d, in_=s),
             inc=(s_g, 16))

    gdma(ident[:, :], identv[:, :])
    gdma(caus[:, :], maskv[:, :])
    gdma(b3_sb[0:1, :], bias3[0:1, :])
    gdma(b3_sb[32:33, :], bias3[1:2, :])
    gdma(b3_sb[64:65, :], bias3[2:3, :])
    gdma(bm_sb[0:1, :], bmv[0:1, :])
    for d in range(DCH):
        gdma(rw_sb[:, 8 * d:8 * d + 8], rwt[d, :, :])
    gdma(sink[:, 9:10], tempv[:, :])
    gdma(sink[:, 0:8], rbv[:, :])
    consts_g = p.cur(s_g)
    p.wait("vector", consts_g)
    for bb in (0, 32, 64):
        ve(lambda v, bb=bb: v.memset(ones3[bb:bb + 1, :], 1.0))
        ve(lambda v, bb=bb: v.memset(ones65[bb:bb + 1, :], 1.0))
    ve(lambda v: v.memset(onecol[:, :], 1.0))
    for i in range(2):
        for base in (0, 16 * 65):
            for tt in range(16):
                mk = ve(lambda v, i=i, base=base, tt=tt:
                        v.memset(v_sb[i][:, base + 65 * tt + 64:base + 65 * tt + 65],
                                 1.0), mark=True)
    consts_v = p.cur(s_v)

    ve(lambda v: v.tensor_scalar_max(sink[:, 10:11], sink[:, 9:10], 0.1))
    ve(lambda v: v.engine_nop())
    ve(lambda v: v.engine_nop())
    ve(lambda v: v.reciprocal(sink[:, 8:9], sink[:, 10:11]))
    ve(lambda v: v.engine_nop())
    ve(lambda v: v.engine_nop())
    mk = ve(lambda v: v.tensor_scalar(rbt_sb[:, 0:8], sink[:, 0:8], sink[:, 8:9],
                                      0.0, ALU.mult, ALU.add), mark=True)
    prep_v = p.cur(s_v)
    rbt_ready = p.cur(s_v)
    B[2].acquire(p, "tensor")
    p.wait("tensor", consts_g)
    p.wait("tensor", consts_v)
    for d in range(DCH):
        mk = mm(pb[2][0:1, 0:8], onecol_r[:, :], rw_r[:, 8 * d:8 * d + 8],
                d == 0, d == DCH - 1, mark=(d == DCH - 1))
    B[2].produce(mk)
    B[2].consume_wait(p, "scalar")
    mk = sc(lambda s: s.copy(tiny2[0:1, 8:16], pb[2][0:1, 0:8]), mark=True)
    B[2].used(mk)
    cs_row_ready = p.cur(s_s)
    B[3].acquire(p, "tensor")
    p.wait("tensor", cs_row_ready)
    mk = mm(pb[3][:, 0:8], ones65[0:1, :], tiny2[0:1, 8:16], True, True, mark=True)
    B[3].produce(mk)
    B[3].consume_wait(p, "scalar")
    mk = sc(lambda s: s.copy(cs_sb[:, 0:8], pb[3][:, 0:8]), mark=True)
    B[3].used(mk)
    cs_ready = p.cur(s_s)

    def xap(d, lo, hi, r=False):
        t = x_r if r else x_sb
        return t[:, S * d + lo:S * d + hi]

    def lap(d, lo, hi):
        return lnT[:, S * d + lo:S * d + hi]

    R_qkv = [Res(), Res()]
    R_v = [Res(), Res()]
    R_probs = [Res(), Res()]
    R_asb = Res()
    R_lnT = Res()
    R_attnT = Res()
    R_arena = Res()
    R_psb = [Res(), Res()]
    R_h = Res()
    wb_res = [Res(), Res()]
    row0_res = [Res(), Res()]
    R_la = Res()
    R_rows = Res()
    R_scr = [Res(), Res()]
    R_x = Res()

    x_dmas = [0]
    y_dmas = [0]
    ystore_cnt = {}
    tapn = [0]

    for b in range(n_batch):
        for d in range(DCH):
            if b > 0:
                p.wait("sync", (s_y, 16 * ystore_cnt[(b - 1, 1, d)]))
            p.op("sync", lambda s, b=b, d=d: s.dma_start(out=xap(d, 0, S), in_=xT[b, d, :, :]),
                 inc=(s_x, 16))
            x_dmas[0] += 1
        x_ready = (s_x, 16 * x_dmas[0])
        if b == TAPB and "xload" in tap_dram:
            p.wait("gpsimd", x_ready)
            p.op("gpsimd", lambda g: g.dma_start(out=tap_dram["xload"][:, :],
                                                 in_=x_sb[:, :]), inc=(s_y, 16))
            y_dmas[0] += 1

        _ln1_s_marks = {}
        for si in range(TS):
            lo, hi = 512 * si, 512 * si + 512
            for d in range(DCH):
                par = d % 2
                R_scr[par].acquire(p, "vector")
                p.wait("vector", x_ready)
                mk = ve(lambda v, d=d, lo=lo, hi=hi, par=par:
                        v.tensor_mul(scr[:, 512 * par:512 * par + 512],
                                     xap(d, lo, hi), xap(d, lo, hi)), mark=True)
                R_scr[par].produce(mk)
                if d == 0:
                    BS.acquire(p, "tensor")
                    B[5].acquire(p, "tensor")
                p.wait("tensor", x_ready)
                mm(pS[0:1, 0:512], onecol_r[:, :], xap(d, lo, hi, r=True),
                   d == 0, d == DCH - 1, mark=(d == DCH - 1))
                R_scr[par].consume_wait(p, "tensor")
                mk = mm(pb[5][0:1, 0:512], onecol_r[:, :],
                        scr_r[:, 512 * par:512 * par + 512],
                        d == 0, d == DCH - 1, mark=True)
                R_scr[par].used(mk)
            BS.produce(p.cur(s_t))
            B[5].produce(mk)
            R_rows.acquire(p, "scalar")
            BS.consume_wait(p, "scalar")
            mk = sc(lambda s, lo=lo: s.mul(rows[0:1, lo:lo + 512], pS[0:1, 0:512],
                                           1.0 / D), mark=True)
            BS.used(mk)
            m_ready = p.cur(s_s)
            B[5].consume_wait(p, "vector")
            p.wait("vector", m_ready)
            ve(lambda v, lo=lo: v.tensor_mul(vrow[0:1, :], rows[0:1, lo:lo + 512],
                                             rows[0:1, lo:lo + 512]))
            mk = ve(lambda v: v.tensor_scalar(vrow2[0:1, :], pb[5][0:1, 0:512],
                                              1.0 / D, 0.0, ALU.mult, ALU.add), mark=True)
            B[5].used(mk)
            mk = ve(lambda v: v.tensor_sub(vrow[0:1, :], vrow2[0:1, :], vrow[0:1, :]),
                    mark=True)
            var_ready = p.cur(s_v)
            p.wait("scalar", var_ready)
            sc(lambda s: s.activation(vrow2[0:1, :], vrow[0:1, :], AF.Ln))
            mk = sc(lambda s, lo=lo: s.activation(rows[32:33, lo:lo + 512],
                                                  vrow2[0:1, :], AF.Exp, scale=-0.5),
                    mark=True)
            R_rows.produce(mk)
            B[3].acquire(p, "tensor")
            B[4].acquire(p, "tensor")
            R_rows.consume_wait(p, "tensor")
            mm(pb[3][:, 0:512], ones65_r[0:1, :], rows_r[0:1, lo:hi], True, True)
            mk = mm(pb[4][:, 0:512], ones65_r[32:33, :], rows_r[32:33, lo:hi],
                    True, True, mark=True)
            B[3].produce(mk)
            B[4].produce(mk)
            R_rows.used(mk)
            B[3].consume_wait(p, "vector")
            for d in range(DCH):
                par = d % 2
                R_scr[par].acquire(p, "vector")
                mk = ve(lambda v, d=d, lo=lo, hi=hi, par=par:
                        v.tensor_sub(scr[:, 512 * par:512 * par + 512],
                                     xap(d, lo, hi), pb[3][:, 0:512]), mark=True)
                R_scr[par].produce(mk)
                mk = ve(lambda v, d=d, lo=lo, hi=hi, par=par:
                        v.tensor_mul(lap(d, lo, hi),
                                     scr[:, 512 * par:512 * par + 512],
                                     pb[4][:, 0:512]), mark=True)
                R_scr[par].used(mk)
                B[4].used(mk)
                B[3].used(mk)
            _ln1_s_marks[si] = p.cur(s_v)
        R_lnT.produce(p.cur(s_v))
        ln1_done_v = p.cur(s_v)
        ln1_s_marks = dict(_ln1_s_marks)

        if "attn" in phases:
            for i in range(2):
                R_v[i].acquire(p, "vector")
                for base in (0, 16 * 65):
                    for tt in range(16):
                        mk = ve(lambda v, i=i, base=base, tt=tt:
                                v.memset(v_sb[i][:, base + 65 * tt + 64:
                                                 base + 65 * tt + 65], 1.0),
                                mark=(tt == 15))
                R_v[i].produce(mk)
            vones_v = p.cur(s_v)
            pair_info = {}

            def prep_pair_chunks(pr):
                par = pr % 2
                chunks = []
                slots, wmark, ck = ring_load(wqkv[pr, 0], DCH)
                slots_k, wmark_k, ck_k = ring_load(wqkv[pr, 1], DCH)
                slots_v, wmark_v, ck_v = ring_load(wqkv[pr, 2], DCH)

                def c_acq():
                    R_qkv[par].acquire(p, "scalar")
                chunks.append(c_acq)

                def proj_chunk(kind, wslots, bias_row, bank, si, wm, ckx, pr=pr,
                               par=par):
                    def go():
                        dst = {"q": q_sb[par], "k": k_sb[par], "v": vt_sb[par]}[kind]
                        p.wait("tensor", wm)
                        p.wait("tensor", ln1_s_marks[si])
                        B[bank].acquire(p, "tensor")
                        for d in range(DCH):
                            mm(pb[bank][:, 0:512], wslots[d],
                               lap(d, 512 * si, 512 * si + 512), d == 0, False)
                        mk = mm(pb[bank][:, 0:512],
                                b3_sb[bias_row:bias_row + 1, 128 * pr:128 * pr + 128],
                                ones3[bias_row:bias_row + 1, :], False, True,
                                mark=True)
                        B[bank].produce(mk)
                        B[bank].consume_wait(p, "scalar")
                        mk = sc(lambda s, dst=dst, si=si, bank=bank:
                                s.copy(dst[:, 512 * si:512 * si + 512],
                                       pb[bank][:, 0:512]), mark=True)
                        B[bank].used(mk)
                        if si == TS - 1:
                            ring_free(ckx, p.cur(s_t))
                    return go

                for si in range(TS):
                    chunks.append(proj_chunk("q", slots, 0, (0, 1)[si % 2], si,
                                             wmark, ck))
                for si in range(TS):
                    chunks.append(proj_chunk("k", slots_k, 32, (2, 3)[si % 2], si,
                                             wmark_k, ck_k))
                for si in range(TS):
                    chunks.append(proj_chunk("v", slots_v, 64, (4, 5)[si % 2], si,
                                             wmark_v, ck_v))

                def c_qkvdone():
                    pair_info[pr] = {"qkv": p.cur(s_s)}
                    R_qkv[par].produce(pair_info[pr]["qkv"])
                chunks.append(c_qkvdone)

                def vt_chunk(t0, par=par, pr=pr):
                    def go():
                        if t0 == 0:
                            R_v[par].acquire(p, "scalar")
                            p.wait("scalar", vones_v)
                        for tt in range(t0, t0 + 4):
                            BT.acquire(p, "tensor")
                            p.wait("tensor", pair_info[pr]["qkv"])
                            mk = tr(pT[:, 0:128],
                                    vt_sb[par][:, 128 * tt:128 * tt + 128], mark=True)
                            BT.produce(mk)
                            BT.consume_wait(p, "scalar")
                            sc(lambda s, par=par, tt=tt:
                               s.copy(v_sb[par][:, 65 * tt:65 * tt + 64], pT[:, 0:64]),
                               mark=True)
                            sc(lambda s, par=par, tt=tt:
                               s.copy(v_sb[par][:, 16 * 65 + 65 * tt:
                                                16 * 65 + 65 * tt + 64],
                                      pT[:, 64:128]), mark=True)
                            BT.used(p.cur(s_s))
                        if t0 == 12:
                            R_v[par].produce(p.cur(s_s))
                    return go

                for t0 in range(0, 16, 4):
                    chunks.append(vt_chunk(t0))

                def c_bound1(par=par, pr=pr):
                    R_qkv[par].consume_wait(p, "vector")
                    R_scr[0].acquire(p, "vector")
                    R_scr[1].acquire(p, "vector")
                    ve(lambda v, par=par: v.tensor_mul(scr[:, 0:S], q_sb[par][:, :],
                                                       q_sb[par][:, :]))
                    ve(lambda v: v.reduce_max(scr2[:, 0:1], scr[:, 0:S], axis=AX.X),
                       mark=True)
                    ve(lambda v, par=par: v.tensor_mul(scr[:, 0:S], k_sb[par][:, :],
                                                       k_sb[par][:, :]))
                    mk2 = ve(lambda v: v.reduce_max(scr2[:, 1:2], scr[:, 0:S],
                                                    axis=AX.X), mark=True)
                    R_qkv[par].used(mk2)
                    R_scr[0].produce(mk2)
                    R_scr[1].produce(mk2)
                chunks.append(c_bound1)

                def c_bound2(par=par, pr=pr):
                    qmx_ready = p.cur(s_v)
                    BS.acquire(p, "tensor")
                    p.wait("tensor", qmx_ready)
                    mm(pS[0:1, 0:1], onecol[0:64, :], scr2[0:64, 0:1], True, True)
                    mm(pS[0:1, 1:2], onecol[64:128, :], scr2[64:128, 0:1], True, True)
                    mm(pS[0:1, 2:3], onecol[0:64, :], scr2[0:64, 1:2], True, True)
                    mk = mm(pS[0:1, 3:4], onecol[64:128, :], scr2[64:128, 1:2],
                            True, True, mark=True)
                    BS.produce(mk)
                    BS.consume_wait(p, "scalar")
                    mk = sc(lambda s: s.copy(tiny[0:1, 4:8], pS[0:1, 0:4]), mark=True)
                    BS.used(mk)
                    p.wait("vector", p.cur(s_s))
                    ve(lambda v: v.tensor_mul(tiny[0:1, 8:9], tiny[0:1, 4:5],
                                              tiny[0:1, 6:7]))
                    ve(lambda v: v.tensor_mul(tiny[0:1, 9:10], tiny[0:1, 5:6],
                                              tiny[0:1, 7:8]), mark=True)
                chunks.append(c_bound2)

                def c_bound3(par=par, pr=pr):
                    p.wait("scalar", p.cur(s_v))
                    sc(lambda s: s.activation(tiny[0:1, 10:12], tiny[0:1, 8:10],
                                              AF.Ln))
                    sc(lambda s: s.copy(tiny[0:1, 4:6], tiny[0:1, 4:6]))
                    sc(lambda s: s.copy(tiny[0:1, 4:6], tiny[0:1, 4:6]))
                    sc(lambda s: s.activation(tiny[0:1, 12:14], tiny[0:1, 10:12],
                                              AF.Exp, scale=0.5))
                    sc(lambda s: s.copy(tiny[0:1, 4:6], tiny[0:1, 4:6]))
                    sc(lambda s: s.copy(tiny[0:1, 4:6], tiny[0:1, 4:6]))
                    sc(lambda s: s.mul(tiny[0:1, 14:16], tiny[0:1, 12:14], -0.125),
                       mark=True)
                    negb_row = p.cur(s_s)
                    BS.acquire(p, "tensor")
                    p.wait("tensor", negb_row)
                    mk = mm(pS[:, 0:2], ones65[0:1, :], tiny[0:1, 14:16], True, True,
                            mark=True)
                    BS.produce(mk)
                    BS.consume_wait(p, "scalar")
                    mk = sc(lambda s, par=par: s.copy(negB[:, 2 * par:2 * par + 2],
                                                      pS[:, 0:2]), mark=True)
                    BS.used(mk)
                    sc(lambda s: s.copy(tiny[0:1, 4:6], tiny[0:1, 4:6]))
                    sc(lambda s: s.copy(tiny[0:1, 4:6], tiny[0:1, 4:6]))
                    pair_info[pr]["negb"] = p.cur(s_s)
                chunks.append(c_bound3)
                return chunks

            def attention_pair(pr, feed):
                par = pr % 2
                qkv_ready_s = pair_info[pr]["qkv"]
                negb_ready = pair_info[pr]["negb"]
                for hh in range(2):
                    hb = 64 * hh
                    vcolbase = 16 * 65 * hh

                    def S_emit(qi, hh=hh, hb=hb, par=par):
                        ppar = qi % 2
                        nk = qi + 1
                        R_probs[ppar].acquire(p, "scalar")
                        for g0 in range(0, nk, 4):
                            gn = min(4, nk - g0)
                            bank = (0, 1, 4)[sgrp[0] % 3]
                            sgrp[0] += 1
                            B[bank].acquire(p, "tensor")
                            p.wait("tensor", qkv_ready_s)
                            for ki in range(gn):
                                kc = g0 + ki
                                diag = (kc == qi)
                                mm(pb[bank][:, 128 * ki:128 * ki + 128],
                                   k_sb[par][hb:hb + 64, 128 * kc:128 * kc + 128],
                                   q_sb[par][hb:hb + 64, 128 * qi:128 * qi + 128],
                                   True, not diag,
                                   mark=(ki == gn - 1 and not diag))
                                if diag:
                                    mm(pb[bank][:, 128 * ki:128 * ki + 128],
                                       ident[:, :], caus[:, :], False, True,
                                       mark=(ki == gn - 1))
                            B[bank].produce(p.cur(s_t))
                            B[bank].consume_wait(p, "scalar")
                            p.wait("scalar", negb_ready)
                            mk = sc(lambda s, ppar=ppar, g0=g0, gn=gn, bank=bank,
                                    hh=hh, par=par:
                                    s.activation(
                                        probs_sb[ppar][:, 128 * g0:128 * g0 + 128 * gn],
                                        pb[bank][:, 0:128 * gn], AF.Exp,
                                        bias=negB[:, 2 * par + hh:2 * par + hh + 1],
                                        scale=0.125),
                                    mark=True)
                            B[bank].used(mk)
                        R_probs[ppar].produce(p.cur(s_s))

                    def PV_emit(qi, hh=hh, par=par, vcolbase=vcolbase):
                        ppar = qi % 2
                        nk = qi + 1
                        bank = 2 + hh
                        B[bank].acquire(p, "tensor")
                        R_probs[ppar].consume_wait(p, "tensor")
                        R_v[par].consume_wait(p, "tensor")
                        for kc in range(nk):
                            mk = mm(pb[bank][:, 0:65],
                                    probs_sb[ppar][:, 128 * kc:128 * kc + 128],
                                    v_sb[par][:, vcolbase + 65 * kc:vcolbase + 65 * kc + 65],
                                    kc == 0, kc == nk - 1, mark=(kc == nk - 1))
                        B[bank].produce(mk)
                        R_probs[ppar].used(mk)
                        R_v[par].used(mk)
                        B[bank].consume_wait(p, "vector")
                        rcol = 2 + 2 * hh + (qi % 2)
                        prevm = rl_slots.get(rcol)
                        if prevm is not None:
                            p.wait("vector", prevm)
                        ve(lambda v, bank=bank, rcol=rcol:
                           v.reciprocal(scr2[:, rcol:rcol + 1], pb[bank][:, 64:65]),
                           mark=True)
                        R_asb.acquire(p, "scalar") if (hh == 0 and qi == 0) else None
                        p.wait("scalar", p.cur(s_v))
                        mk = sc(lambda s, bank=bank, qi=qi, hh=hh, rcol=rcol:
                                s.activation(asb[:, 128 * qi + 64 * hh:
                                                 128 * qi + 64 * hh + 64],
                                             pb[bank][:, 0:64], AF.Copy,
                                             scale=scr2[:, rcol:rcol + 1]), mark=True)
                        B[bank].used(mk)
                        rl_slots[rcol] = mk

                    sgrp = [0]
                    rl_slots = {}
                    S_emit(0)
                    feed()
                    for qi in range(1, NQT):
                        S_emit(qi)
                        feed()
                        PV_emit(qi - 1)
                        feed()
                    PV_emit(NQT - 1)
                    feed()

            pending = prep_pair_chunks(0)
            while pending:
                pending.pop(0)()
            for pr in range(NP):
                par = pr % 2
                pending = prep_pair_chunks(pr + 1) if pr + 1 < NP else []

                def feed():
                    if pending:
                        pending.pop(0)()
                attention_pair(pr, feed)
                while pending:
                    pending.pop(0)()
                asb_ready = p.cur(s_s)
                R_asb.produce(asb_ready)
                for tt in range(16):
                    BT.acquire(p, "tensor")
                    p.wait("tensor", asb_ready)
                    mk = tr(pT[:, 0:128], asb[:, 128 * tt:128 * tt + 128], mark=True)
                    BT.produce(mk)
                    R_asb.used(mk)
                    BT.consume_wait(p, "scalar")
                    mk = sc(lambda s, pr=pr, tt=tt:
                            s.copy(attnT[:, S * pr + 128 * tt:S * pr + 128 * tt + 128],
                                   pT[:, 0:128]), mark=True)
                    BT.used(mk)
            R_attnT.produce(p.cur(s_s))

            for dm in range(DCH):
                slots, wmark, ck = ring_load(wm[dm], DCH)
                for si in range(TS):
                    bank = si % 2
                    B[bank].acquire(p, "tensor")
                    p.wait("tensor", wmark)
                    R_attnT.consume_wait(p, "tensor")
                    for d in range(DCH):
                        mm(pb[bank][:, 0:512], slots[d],
                           attnT[:, S * d + 512 * si:S * d + 512 * si + 512],
                           d == 0, False)
                    mk = mm(pb[bank][:, 0:512], bm_sb[0:1, 128 * dm:128 * dm + 128],
                            ones3[0:1, :], False, True, mark=True)
                    B[bank].produce(mk)
                    if si == TS - 1:
                        R_attnT.used(mk)
                        ring_free(ck, p.cur(s_t))
                    B[bank].consume_wait(p, "vector")
                    mk = ve(lambda v, dm=dm, si=si, bank=bank:
                            v.tensor_add(xap(dm, 512 * si, 512 * si + 512),
                                         xap(dm, 512 * si, 512 * si + 512),
                                         pb[bank][:, 0:512]), mark=True)
                    B[bank].used(mk)
            wm_done_v = p.cur(s_v)

        if "ffn" not in phases:
            xdone = p.cur(s_v)
            for d in range(DCH):
                p.wait("sync", xdone)
                p.op("sync",
                     lambda s, b=b, d=d: s.dma_start(out=y[b, d, :, :], in_=xap(d, 0, S)),
                     inc=(s_y, 16))
                y_dmas[0] += 1
            continue

        for si in range(TS):
            lo, hi = 512 * si, 512 * si + 512
            for d in range(DCH):
                par = d % 2
                R_scr[par].acquire(p, "vector")
                mk = ve(lambda v, d=d, lo=lo, hi=hi, par=par:
                        v.tensor_mul(scr[:, 512 * par:512 * par + 512],
                                     xap(d, lo, hi), xap(d, lo, hi)), mark=True)
                R_scr[par].produce(mk)
                if d == 0:
                    BS.acquire(p, "tensor")
                    B[5].acquire(p, "tensor")
                    if "attn" in phases:
                        p.wait("tensor", wm_done_v)
                mm(pS[0:1, 0:512], onecol_r[:, :], xap(d, lo, hi, r=True),
                   d == 0, d == DCH - 1, mark=(d == DCH - 1))
                R_scr[par].consume_wait(p, "tensor")
                mk = mm(pb[5][0:1, 0:512], onecol_r[:, :],
                        scr_r[:, 512 * par:512 * par + 512],
                        d == 0, d == DCH - 1, mark=True)
                R_scr[par].used(mk)
            BS.produce(p.cur(s_t))
            B[5].produce(mk)
            R_rows.acquire(p, "scalar")
            BS.consume_wait(p, "scalar")
            mk = sc(lambda s, lo=lo: s.mul(rows[0:1, lo:lo + 512], pS[0:1, 0:512],
                                           1.0 / D), mark=True)
            BS.used(mk)
            m_ready = p.cur(s_s)
            B[5].consume_wait(p, "vector")
            p.wait("vector", m_ready)
            ve(lambda v, lo=lo: v.tensor_mul(vrow[0:1, :], rows[0:1, lo:lo + 512],
                                             rows[0:1, lo:lo + 512]))
            mk = ve(lambda v: v.tensor_scalar(vrow2[0:1, :], pb[5][0:1, 0:512],
                                              1.0 / D, 0.0, ALU.mult, ALU.add), mark=True)
            B[5].used(mk)
            ve(lambda v: v.scalar_tensor_tensor(vrow[0:1, :], vrow2[0:1, :], EPS,
                                                vrow[0:1, :], ALU.add, ALU.subtract),
               mark=True)
            var_ready = p.cur(s_v)
            p.wait("scalar", var_ready)
            sc(lambda s: s.activation(vrow2[0:1, :], vrow[0:1, :], AF.Ln))
            mk = sc(lambda s, lo=lo: s.activation(rows[32:33, lo:lo + 512],
                                                  vrow2[0:1, :], AF.Exp, scale=-0.5),
                    mark=True)
            R_rows.produce(mk)
        ln2_stats_s = p.cur(s_s)

        BS.acquire(p, "tensor")
        p.wait("tensor", ln2_stats_s)
        for tt in range(16):
            mm(pS[:, tt:tt + 1], rows[0:1, 128 * tt:128 * tt + 128],
               onecol[0:1, 0:1], True, True)
        for tt in range(16):
            mk = mm(pS[:, 16 + tt:17 + tt], rows[32:33, 128 * tt:128 * tt + 128],
                    onecol[32:33, 0:1], True, True, mark=(tt == 15))
        BS.produce(mk)
        BS.consume_wait(p, "scalar")
        mk = sc(lambda s: s.copy(mcols[:, 0:16], pS[:, 0:16]), mark=True)
        mk = sc(lambda s: s.copy(scr2[:, 8:24], pS[:, 16:32]), mark=True)
        BS.used(mk)
        p.wait("vector", p.cur(s_s))
        p.wait("vector", rbt_ready)
        mk = ve(lambda v: v.tensor_scalar(mcols[:, 16:32], scr2[:, 8:24],
                                          sink[:, 8:9], -1.0, ALU.mult, ALU.mult),
                mark=True)
        cols_ready = p.cur(s_v)
        R_la.acquire(p, "vector")
        R_scr[0].acquire(p, "scalar")
        R_scr[1].acquire(p, "scalar")
        for tt in range(16):
            bank = tt % 2
            B[bank].acquire(p, "tensor")
            for d in range(DCH):
                mk = mm(pb[bank][:, 0:8], xap(d, 128 * tt, 128 * tt + 128, r=True),
                        rw_r[:, 8 * d:8 * d + 8], d == 0, d == DCH - 1,
                        mark=(d == DCH - 1))
            B[bank].produce(mk)
            B[bank].consume_wait(p, "scalar")
            mk = sc(lambda s, tt=tt, bank=bank:
                    s.copy(scr[:, 256 + 8 * tt:256 + 8 * tt + 8], pb[bank][:, 0:8]),
                    mark=True)
            B[bank].used(mk)
        raw_s = p.cur(s_s)
        p.wait("vector", raw_s)
        p.wait("vector", cols_ready)
        p.wait("vector", cs_ready)
        p.wait("vector", rbt_ready)
        for tt in range(16):
            ve(lambda v, tt=tt: v.scalar_tensor_tensor(
                scr[:, 384 + 8 * tt:384 + 8 * tt + 8], cs_sb[:, 0:8],
                mcols[:, tt:tt + 1], scr[:, 256 + 8 * tt:256 + 8 * tt + 8],
                ALU.mult, ALU.subtract))
        for tt in range(16):
            ve(lambda v, tt=tt: v.tensor_scalar(
                scr[:, 512 + 8 * tt:512 + 8 * tt + 8],
                scr[:, 384 + 8 * tt:384 + 8 * tt + 8],
                mcols[:, 16 + tt:17 + tt], 0.0, ALU.mult, ALU.add))
        for tt in range(16):
            mk = ve(lambda v, tt=tt: v.tensor_add(
                la_sb[:, 8 * tt:8 * tt + 8], scr[:, 512 + 8 * tt:512 + 8 * tt + 8],
                rbt_sb[:, 0:8]), mark=(tt == 15))
        R_scr[0].produce(mk)
        R_scr[1].produce(mk)
        R_la.produce(p.cur(s_v))
        if b == TAPB and "la0" in tap_dram:
            p.wait("gpsimd", p.cur(s_v))
            p.op("gpsimd", lambda g: g.dma_start(out=tap_dram["la0"][:, :],
                                                 in_=la_sb[:, :]), inc=(s_y, 16))
            y_dmas[0] += 1
        if b == TAPB and "mcols0" in tap_dram:
            p.wait("gpsimd", p.cur(s_v))
            p.op("gpsimd", lambda g: g.dma_start(out=tap_dram["mcols0"][:, :],
                                                 in_=mcols[:, :]), inc=(s_y, 16))
            p.op("gpsimd", lambda g: g.dma_start(out=tap_dram["cs0"][:, :],
                                                 in_=cs_sb[:, :]), inc=(s_y, 16))
            p.op("gpsimd", lambda g: g.dma_start(out=tap_dram["rbt0"][:, :],
                                                 in_=rbt_sb[:, :]), inc=(s_y, 16))
            p.op("gpsimd", lambda g: g.dma_start(out=tap_dram["tiny0"][:, :],
                                                 in_=tiny[0:1, :]), inc=(s_y, 16))
            p.op("gpsimd", lambda g: g.dma_start(out=tap_dram["rtcol0"][:, :],
                                                 in_=sink[:, 8:9]), inc=(s_y, 16))
            y_dmas[0] += 5
        for it in range(3):
            for tt in range(16):
                ve(lambda v, tt=tt: v.reduce_max(scr[:, 1664 + tt:1665 + tt],
                                                 la_sb[:, 8 * tt:8 * tt + 8],
                                                 axis=AX.X))
            for tt in range(16):
                mk = ve(lambda v, tt=tt: v.tensor_scalar_mul(
                    scr[:, 1680 + tt:1681 + tt], scr[:, 1664 + tt:1665 + tt], -1.0),
                    mark=(tt == 15))
            rowmax_v = p.cur(s_v)
            p.wait("scalar", rowmax_v)
            for tt in range(16):
                sc(lambda s, tt=tt: s.activation(
                    scr[:, 8 * tt:8 * tt + 8], la_sb[:, 8 * tt:8 * tt + 8], AF.Exp,
                    bias=scr[:, 1680 + tt:1681 + tt],
                    accum_out=scr[:, 1728 + tt:1729 + tt]))
            for tt in range(16):
                mk = sc(lambda s, tt=tt: s.activation(
                    scr[:, 1712 + tt:1713 + tt], scr[:, 1728 + tt:1729 + tt], AF.Ln),
                    mark=(tt == 15))
            lse_s = p.cur(s_s)
            p.wait("vector", lse_s)
            for tt in range(16):
                mk = ve(lambda v, tt=tt: v.tensor_scalar(
                    la_sb[:, 8 * tt:8 * tt + 8], la_sb[:, 8 * tt:8 * tt + 8],
                    scr[:, 1664 + tt:1665 + tt], scr[:, 1712 + tt:1713 + tt],
                    ALU.subtract, ALU.subtract), mark=(tt == 15))
            rowstep_v = p.cur(s_v)
            p.wait("scalar", rowstep_v)
            for tt in range(16):
                mk = sc(lambda s, tt=tt:
                        s.activation(scr[:, 8 * tt:8 * tt + 8],
                                     la_sb[:, 8 * tt:8 * tt + 8], AF.Exp),
                        mark=True)
                if tt == 0:
                    BS.acquire(p, "tensor")
                p.wait("tensor", p.cur(s_s))
                mk = mm(pS[0:1, 0:8], onecol[:, :], scr[:, 8 * tt:8 * tt + 8],
                        tt == 0, tt == 15, mark=(tt == 15))
            BS.produce(mk)
            BS.consume_wait(p, "scalar")
            mk = sc(lambda s: s.activation(tiny2[0:1, 16:24], pS[0:1, 0:8], AF.Ln),
                    mark=True)
            BS.used(mk)
            B[4].acquire(p, "tensor")
            p.wait("tensor", p.cur(s_s))
            mk = mm(pb[4][:, 0:8], ones65[0:1, :], tiny2[0:1, 16:24], True, True,
                    mark=True)
            B[4].produce(mk)
            B[4].consume_wait(p, "vector")
            for tt in range(16):
                mk = ve(lambda v, tt=tt: v.tensor_sub(la_sb[:, 8 * tt:8 * tt + 8],
                                                      la_sb[:, 8 * tt:8 * tt + 8],
                                                      pb[4][:, 0:8]),
                        mark=(tt == 15))
            B[4].used(mk)
        sink_v = p.cur(s_v)
        p.wait("scalar", sink_v)
        for tt in range(16):
            mk = sc(lambda s, tt=tt: s.activation(
                scr[:, 8 * tt:8 * tt + 8], la_sb[:, 8 * tt:8 * tt + 8], AF.Exp,
                accum_out=scr[:, 1728 + tt:1729 + tt]), mark=(tt == 15))
        gates_s = p.cur(s_s)
        p.wait("vector", gates_s)
        for tt in range(16):
            ve(lambda v, tt=tt: v.tensor_scalar(scr[:, 1744 + tt:1745 + tt],
                                                scr[:, 1728 + tt:1729 + tt], 1e-8,
                                                0.0, ALU.add, ALU.add))
        for tt in range(16):
            ve(lambda v, tt=tt: v.reciprocal(scr[:, 1760 + tt:1761 + tt],
                                             scr[:, 1744 + tt:1745 + tt]))
        for tt in range(16):
            ve(lambda v, tt=tt: v.tensor_scalar(scr[:, 128 + 8 * tt:128 + 8 * tt + 8],
                                                scr[:, 8 * tt:8 * tt + 8],
                                                scr[:, 1760 + tt:1761 + tt], 0.0,
                                                ALU.mult, ALU.add))
        for tt in range(16):
            ve(lambda v, tt=tt: v.max(out=scr[:, 256 + 8 * tt:256 + 8 * tt + 8],
                                      in_=scr[:, 128 + 8 * tt:128 + 8 * tt + 8]))
        for tt in range(16):
            ve(lambda v, tt=tt: v.tensor_scalar(scr[:, 1664 + tt:1665 + tt],
                                                scr[:, 256 + 8 * tt:256 + 8 * tt + 1],
                                                scr[:, 256 + 8 * tt + 1:256 + 8 * tt + 2],
                                                1e-8, ALU.add, ALU.add))
        for tt in range(16):
            ve(lambda v, tt=tt: v.reciprocal(scr[:, 1680 + tt:1681 + tt],
                                             scr[:, 1664 + tt:1665 + tt]))
        for tt in range(16):
            ve(lambda v, tt=tt: v.tensor_scalar(scr[:, 384 + 8 * tt:384 + 8 * tt + 8],
                                                scr[:, 128 + 8 * tt:128 + 8 * tt + 8],
                                                scr[:, 256 + 8 * tt + 1:256 + 8 * tt + 2],
                                                1.0, ALU.is_ge, ALU.mult))
        for tt in range(16):
            mk = ve(lambda v, tt=tt: v.scalar_tensor_tensor(
                scr[:, 512 + 8 * tt:512 + 8 * tt + 8],
                scr[:, 128 + 8 * tt:128 + 8 * tt + 8],
                scr[:, 1680 + tt:1681 + tt],
                scr[:, 384 + 8 * tt:384 + 8 * tt + 8],
                ALU.mult, ALU.mult), mark=(tt == 15))
        wbv_v = p.cur(s_v)
        p.wait("scalar", wbv_v)
        for tt in range(16):
            mk = sc(lambda s, tt=tt: s.copy(wbt_sb[:, 8 * tt:8 * tt + 8],
                                            scr[:, 512 + 8 * tt:512 + 8 * tt + 8]),
                    mark=(tt == 15))
        wbt_ready = p.cur(s_s)
        for rr in R_scr:
            rr.produce(wbv_v)
            rr.used(wbt_ready)

        if "ffn" in phases:
            R_arena.acquire(p, "vector")
        for w in range(NW):
            wlo = 1024 * w
            for si in range(WTS):
                lo = wlo + 512 * si
                hi = lo + 512
                B[3].acquire(p, "tensor")
                B[4].acquire(p, "tensor")
                p.wait("tensor", ln2_stats_s)
                mm(pb[3][:, 0:512], ones65_r[0:1, :], rows_r[0:1, lo:hi], True, True)
                mk = mm(pb[4][:, 0:512], ones65_r[32:33, :], rows_r[32:33, lo:hi],
                        True, True, mark=True)
                B[3].produce(mk)
                B[4].produce(mk)
                R_rows.used(mk)
                B[3].consume_wait(p, "vector")
                B[4].consume_wait(p, "vector")
                for d in range(DCH):
                    par = d % 2
                    R_scr[par].acquire(p, "vector")
                    mk = ve(lambda v, d=d, lo=lo, hi=hi, par=par:
                            v.tensor_sub(scr[:, 512 * par:512 * par + 512],
                                         xap(d, lo, hi), pb[3][:, 0:512]), mark=True)
                    R_scr[par].produce(mk)
                    mk = ve(lambda v, d=d, lo=lo, hi=hi, par=par:
                            v.tensor_mul(scr2[:, 0:512],
                                         scr[:, 512 * par:512 * par + 512],
                                         pb[4][:, 0:512]), mark=True)
                    R_scr[par].used(mk)
                    ve(lambda v, d=d, lo=lo, hi=hi:
                       v.tensor_add(xap(d, lo, hi), xap(d, lo, hi), scr2[:, 0:512]))
                    mk = ve(lambda v, d=d, lo=lo, hi=hi:
                            v.tensor_copy(lap(d, lo, hi), scr2[:, 0:512]), mark=True)
                    B[3].used(mk)
                    B[4].used(mk)
            ln2w_v = p.cur(s_v)
            R_lnT.produce(ln2w_v)

            if "ffn" not in phases:
                continue
            R_wb2 = wb_res
            def wb_part1(e):
                par = e % 2
                BS.acquire(p, "tensor")
                p.wait("tensor", wbt_ready)
                rowm = row0_res[par]
                for m2 in rowm.users:
                    p.wait("tensor", m2)
                for j in range(8):
                    tt = 8 * w + j
                    mk = mm(pS[0:1, 128 * (j % 4):128 * (j % 4) + 128],
                            wbt_sb[:, 8 * tt + e:8 * tt + e + 1], ident[:, :],
                            True, True, mark=(j % 4 == 3))
                    if j % 4 == 3:
                        BS.produce(mk)
                        BS.consume_wait(p, "scalar")
                        mk = sc(lambda s, j=j, par=par:
                                s.copy(row0p[par][0:1, 512 * (j // 4):512 * (j // 4) + 512],
                                       pS[0:1, 0:512]), mark=True)
                        BS.used(mk)
                        if j == 3:
                            BS.acquire(p, "tensor")
                rowm.produce(p.cur(s_s))

            def wb_part2(e):
                par = e % 2
                R_wb2[par].acquire(p, "scalar")
                for si in range(WTS):
                    bank = 4 + si
                    B[bank].acquire(p, "tensor")
                    row0_res[par].consume_wait(p, "tensor")
                    mk = mm(pb[bank][:, 0:512], ones3[0:1, 0:128],
                            row0p[par][0:1, 512 * si:512 * si + 512], True, True,
                            mark=True)
                    B[bank].produce(mk)
                    row0_res[par].used(mk)
                    B[bank].consume_wait(p, "scalar")
                    mk = sc(lambda s, si=si, bank=bank, par=par:
                            s.copy(wb2[par][:, 512 * si:512 * si + 512],
                                   pb[bank][:, 0:512]), mark=True)
                    B[bank].used(mk)
                R_wb2[par].produce(p.cur(s_s))

            wb_part1(0)
            wb_part2(0)
            wo_last_marks = {}
            for e in range(NE):
                wb_sb = wb2[e % 2]
                R_wb = R_wb2[e % 2]
                h_marks = {}
                for f in range(FCH):
                    sg, wg_m, ckg = ring_load(ffn[e, 16 * f:16 * f + 8], 8)
                    sp_, wp_m, ckp = ring_load(ffn[e, 16 * f + 8:16 * f + 16], 8)
                    gbanks = (0, 1) if f % 2 == 0 else (2, 3)
                    p.wait("tensor", wg_m)
                    R_lnT.consume_wait(p, "tensor")
                    for si in range(WTS):
                        B[gbanks[si]].acquire(p, "tensor")
                    for si in range(WTS):
                        for d in range(DCH):
                            mm(pb[gbanks[si]][:, 0:512], sg[d],
                               lap(d, wlo + 512 * si, wlo + 512 * si + 512),
                               d == 0, d == DCH - 1,
                               mark=(d == DCH - 1))
                    gmk = p.cur(s_t)
                    for si in range(WTS):
                        B[gbanks[si]].produce(gmk)
                    ring_free(ckg, gmk)
                    p.wait("tensor", wp_m)
                    pbanks = (4, 5)
                    for si in range(WTS):
                        B[pbanks[si]].acquire(p, "tensor")
                    for si in range(WTS):
                        for d in range(DCH):
                            mm(pb[pbanks[si]][:, 0:512], sp_[d],
                               lap(d, wlo + 512 * si, wlo + 512 * si + 512),
                               d == 0, d == DCH - 1,
                               mark=(d == DCH - 1))
                    pmk = p.cur(s_t)
                    for si in range(WTS):
                        B[pbanks[si]].produce(pmk)
                    ring_free(ckp, pmk)
                    psbpar = f % 2
                    R_psb[psbpar].acquire(p, "scalar")
                    for si in range(WTS):
                        B[pbanks[si]].consume_wait(p, "scalar")
                        mk = sc(lambda s, psbpar=psbpar, si=si, bank=pbanks[si]:
                                s.copy(p_sb[:, 1024 * psbpar + 512 * si:
                                            1024 * psbpar + 512 * si + 512],
                                       pb[bank][:, 0:512]), mark=True)
                        B[pbanks[si]].used(mk)
                    R_psb[psbpar].produce(p.cur(s_s))
                    R_psb[psbpar].consume_wait(p, "vector")
                    R_wb.consume_wait(p, "vector")
                    for si in range(WTS):
                        B[gbanks[si]].consume_wait(p, "vector")
                        R_h.acquire(p, "vector") if (f == 0 and si == 0) else None
                        ve(lambda v, f=f, si=si, gb=gbanks[si], psbpar=psbpar:
                           v.scalar_tensor_tensor(
                               h_sb[:, 1024 * f + 512 * si:1024 * f + 512 * si + 512],
                               pb[gb][:, 0:512], 0.0,
                               p_sb[:, 1024 * psbpar + 512 * si:1024 * psbpar + 512 * si + 512],
                               ALU.max, ALU.mult))
                        mk = ve(lambda v, f=f, si=si, wb_sb=wb_sb:
                                v.tensor_mul(
                                    h_sb[:, 1024 * f + 512 * si:1024 * f + 512 * si + 512],
                                    h_sb[:, 1024 * f + 512 * si:1024 * f + 512 * si + 512],
                                    wb_sb[:, 512 * si:512 * si + 512]), mark=True)
                        B[gbanks[si]].used(mk)
                        mkpsb = mk
                    h_marks[f] = mkpsb
                    R_psb[psbpar].used(mkpsb)
                    R_wb.used(mkpsb)
                R_h.produce(h_marks[FCH - 1])

                if e + 1 < NE:
                    wb_part1(e + 1)
                for do in range(DCH):
                    s1, m1, c1 = ring_load(ffn[e, 256 + 16 * do:256 + 16 * do + 8], 8)
                    s2, m2, c2 = ring_load(ffn[e, 256 + 16 * do + 8:256 + 16 * do + 16], 8)
                    slots_o, womark = s1 + s2, m2
                    obanks = (0, 1) if do % 2 == 0 else (2, 3)
                    p.wait("tensor", womark)
                    for si in range(WTS):
                        B[obanks[si]].acquire(p, "tensor")
                    for si in range(WTS):
                        for fi in range(FCH):
                            p.wait("tensor", h_marks[fi])
                            mm(pb[obanks[si]][:, 0:512], slots_o[fi],
                               h_sb[:, 1024 * fi + 512 * si:1024 * fi + 512 * si + 512],
                               fi == 0, fi == FCH - 1,
                               mark=(fi == FCH - 1))
                    omk = p.cur(s_t)
                    for si in range(WTS):
                        B[obanks[si]].produce(omk)
                    R_h.used(omk)
                    ring_free(c1, omk)
                    ring_free(c2, omk)
                    for si in range(WTS):
                        B[obanks[si]].consume_wait(p, "vector")
                        mk = ve(lambda v, do=do, si=si, bank=obanks[si], wlo=wlo:
                                v.tensor_add(xap(do, wlo + 512 * si, wlo + 512 * si + 512),
                                             xap(do, wlo + 512 * si, wlo + 512 * si + 512),
                                             pb[bank][:, 0:512]), mark=True)
                        B[obanks[si]].used(mk)
                    if e == NE - 1:
                        wo_last_marks[do] = mk
                if e + 1 < NE:
                    wb_part2(e + 1)
            ffn_w_done = p.cur(s_v)
            for d in range(DCH):
                p.wait("sync", wo_last_marks[d])
                p.op("sync",
                     lambda s, b=b, d=d, wlo=wlo:
                     s.dma_start(out=y[b, d, :, wlo:wlo + 1024],
                                 in_=xap(d, wlo, wlo + 1024)),
                     inc=(s_y, 16))
                y_dmas[0] += 1
                ystore_cnt[(b, w, d)] = y_dmas[0]
        if "ffn" in phases:
            R_arena.produce(p.cur(s_v))

    for tname, shape in taps:
        if tname in ("la0", "mcols0", "cs0", "rbt0", "tiny0", "rtcol0", "xload"):
            continue
        src = {"lnT": lambda: lnT, "attnT": lambda: attnT, "x": lambda: x_sb,
               "la": lambda: la_sb, "wbt": lambda: wbt_sb, "h": lambda: h_sb,
               "q0": lambda: q_sb[0], "k0": lambda: k_sb[0], "v0": lambda: v_sb[0],
               "vt0": lambda: vt_sb[0], "probs0": lambda: probs_sb[0],
               "rows": lambda: rows, "asb": lambda: asb}[tname]()
        p.wait("gpsimd", p.cur(s_v))
        p.wait("gpsimd", p.cur(s_s))
        p.wait("gpsimd", p.cur(s_t))
        pn, fn_ = shape
        p.op("gpsimd", lambda g, d=tap_dram[tname], a=src, pn=pn, fn_=fn_:
             g.dma_start(out=d[:, :], in_=a[0:pn, 0:fn_]), inc=(s_y, 16))
        y_dmas[0] += 1

    p.streams["sync"].append(("w", p.sems[s_y], 16 * y_dmas[0]))

    with nc.Block() as block:
        def run(eng_name):
            def body(eng):
                for item in p.streams[eng_name]:
                    if item[0] == "w":
                        eng.wait_ge(item[1], item[2])
                    else:
                        instr = item[1](eng)
                        if item[2] is not None:
                            instr.then_inc(item[2], item[3])
            return body
        block.tensor(run("tensor"))
        block.scalar(run("scalar"))
        block.vector(run("vector"))
        block.gpsimd(run("gpsimd"))
        block.sync(run("sync"))
    return nc



def _tile_w(w):
    Dd, Ff = w.shape
    return np.ascontiguousarray(
        w.reshape(Dd // 128, 128, Ff // 128, 128).transpose(2, 0, 1, 3))


def prepare_inputs(inputs, n_batch_per_core=NB, n_cores=8):
    L = inputs
    x = np.asarray(L["L0_attn_x"], np.float32)
    wq = _tile_w(np.asarray(L["L0_attn_Wq"], np.float32))
    wk = _tile_w(np.asarray(L["L0_attn_Wk"], np.float32))
    wv = _tile_w(np.asarray(L["L0_attn_Wv"], np.float32))
    wqkv = np.ascontiguousarray(np.stack([wq, wk, wv], axis=1))
    wmt = _tile_w(np.asarray(L["L0_attn_Wm"], np.float32))
    bias3 = np.ascontiguousarray(np.stack([
        np.asarray(L["L0_attn_bq"], np.float32),
        np.asarray(L["L0_attn_bk"], np.float32),
        np.asarray(L["L0_attn_bv"], np.float32)]))
    bmv = np.asarray(L["L0_attn_bm"], np.float32).reshape(1, D)
    wg = np.asarray(L["L0_ffn_wg"], np.float32)
    wp = np.asarray(L["L0_ffn_wp"], np.float32)
    wo = np.asarray(L["L0_ffn_wo"], np.float32)
    ffn = np.empty((NE, 384, 128, 128), np.float32)
    for e in range(NE):
        g_t = _tile_w(wg[e])
        p_t = _tile_w(wp[e])
        o_t = _tile_w(wo[e])
        blk = []
        for f in range(FCH):
            blk.append(g_t[f])
            blk.append(p_t[f])
        for do in range(DCH):
            blk.append(o_t[do])
        ffn[e] = np.concatenate(blk, axis=0)
    rwt = np.ascontiguousarray(
        np.asarray(L["L0_ffn_router_w"], np.float32).T.reshape(DCH, 128, 8))
    rbv = np.ascontiguousarray(np.broadcast_to(
        np.asarray(L["L0_ffn_router_b"], np.float32).reshape(1, 8), (128, 8)))
    tempv = np.ascontiguousarray(np.broadcast_to(
        np.asarray(L["L0_ffn_temperature"], np.float32).reshape(1, 1), (128, 1)))
    identv = np.eye(128, dtype=np.float32)
    kk = np.arange(128)[:, None]
    qq = np.arange(128)[None, :]
    maskv = np.where(kk > qq, -1000.0, 0.0).astype(np.float32)

    shared = dict(wqkv=wqkv, wm=wmt, ffn=ffn, bias3=bias3, bmv=bmv, rwt=rwt,
                  rbv=rbv, tempv=tempv, identv=identv, maskv=maskv)
    in_maps = []
    nb = n_batch_per_core
    for c in range(n_cores):
        xc = x[c * nb:(c + 1) * nb]
        xT = np.ascontiguousarray(xc.transpose(0, 2, 1)).reshape(nb, DCH, 128, S)
        in_maps.append(dict(shared, xT=xT))
    return in_maps


def kernel(**inputs):
    nc = build_program()
    in_maps = prepare_inputs(inputs)
    res = run_bass_kernel_spmd(nc, in_maps, core_ids=list(range(8)))
    outs = []
    for c in range(8):
        yc = res.results[c]["y"]
        yc = yc.reshape(NB, D, S).transpose(0, 2, 1)
        outs.append(yc)
    return np.ascontiguousarray(np.concatenate(outs, axis=0), dtype=np.float32)


# version 06
# speedup vs baseline: 1.1813x; 1.1813x over previous
import numpy as np
from contextlib import ExitStack

import concourse.bass as bass
import concourse.mybir as mybir
from concourse.bass_utils import run_bass_kernel_spmd

F32 = mybir.dt.float32
F32R = mybir.dt.float32r
BF16 = mybir.dt.bfloat16
AF = mybir.ActivationFunctionType
ALU = mybir.AluOpType
AX = mybir.AxisListType

D = 1024
S = 2048
DCH = 8
NP = 8
HS = 64
TS = 4
NE = 8
FF = 2048
FCH = 16
NW = 2
WTS = 2
NQT = 16
NB = 4
EPS = 1e-5

ENGINES = ("tensor", "scalar", "vector", "gpsimd", "sync")


class Prog:

    def __init__(self, nc):
        self.nc = nc
        self.streams = {e: [] for e in ENGINES}
        self.sems = {}
        self.counts = {}
        self.waited = {e: {} for e in ENGINES}
        self.ctx = None

    def sem(self, name):
        h = self.ctx.enter_context(self.nc.semaphore(name))
        self.sems[name] = h
        self.counts[name] = 0
        return name

    def wait(self, eng, mark):
        if mark is None:
            return
        sem, val = mark
        if val <= 0:
            return
        if self.waited[eng].get(sem, 0) >= val:
            return
        self.waited[eng][sem] = val
        h = self.sems[sem]
        self.streams[eng].append(("w", h, val))

    def op(self, eng, fn, inc=None):
        if inc is not None:
            sem, amt = inc
            self.counts[sem] += amt
            self.streams[eng].append(("o", fn, self.sems[sem], amt))
            return (sem, self.counts[sem])
        self.streams[eng].append(("o", fn, None, 0))
        return None

    def cur(self, sem):
        return (sem, self.counts[sem])


class Res:

    def __init__(self):
        self.ready = None
        self.users = []

    def acquire(self, prog, eng):
        prog.wait(eng, self.ready)
        for m in self.users:
            prog.wait(eng, m)

    def produce(self, mark):
        self.ready = mark
        self.users = []

    def consume_wait(self, prog, eng):
        prog.wait(eng, self.ready)

    def used(self, mark):
        if mark is not None:
            self.users.append(mark)


TAPB = 0


def build_program(n_batch=NB, taps=(), phases=("attn", "ffn")):
    nc = bass.Bass("TRN2", target_bir_lowering=False)
    p = Prog(nc)

    xT = nc.dram_tensor("xT", [n_batch, DCH, 128, S], F32, kind="ExternalInput").ap()
    wqkv = nc.dram_tensor("wqkv", [NP, 3, DCH, 128, 128], F32, kind="ExternalInput").ap()
    wm = nc.dram_tensor("wm", [DCH, DCH, 128, 128], F32, kind="ExternalInput").ap()
    ffn = nc.dram_tensor("ffn", [NE, 384, 128, 128], F32, kind="ExternalInput").ap()
    bias3 = nc.dram_tensor("bias3", [3, D], F32, kind="ExternalInput").ap()
    bmv = nc.dram_tensor("bmv", [1, D], F32, kind="ExternalInput").ap()
    rwt = nc.dram_tensor("rwt", [DCH, 128, 8], F32, kind="ExternalInput").ap()
    rbv = nc.dram_tensor("rbv", [128, 8], F32, kind="ExternalInput").ap()
    tempv = nc.dram_tensor("tempv", [128, 1], F32, kind="ExternalInput").ap()
    identv = nc.dram_tensor("identv", [128, 128], F32, kind="ExternalInput").ap()
    maskv = nc.dram_tensor("maskv", [128, 128], F32, kind="ExternalInput").ap()
    y = nc.dram_tensor("y", [n_batch, DCH, 128, S], F32, kind="ExternalOutput").ap()
    tap_dram = {}
    for tname, shape in taps:
        tap_dram[tname] = nc.dram_tensor("tap_" + tname, list(shape), F32,
                                         kind="ExternalOutput").ap()

    SBUF_BASE = 16512
    cursor = [SBUF_BASE]

    def alloc(name, shape, dtype, at=None):
        nbytes = {F32: 4, F32R: 4, BF16: 2}[dtype] * shape[-1]
        if at is None:
            at = cursor[0]
            cursor[0] = at + ((nbytes + 31) // 32) * 32
        t = nc.alloc_sbuf_tensor_at(name, list(shape), dtype, offset=at)
        return t, at

    x_sb, x_off = alloc("x", (128, DCH * S), F32)
    x_r, _ = alloc("x_r", (128, DCH * S), F32R, at=x_off)
    lnT, _ = alloc("lnT", (128, DCH * S), BF16)
    attnT, _ = alloc("attnT", (128, DCH * S), BF16)

    arena_base = cursor[0]
    qkv_off = [arena_base + i * (3 * S * 2) for i in range(2)]
    v_off = [arena_base + 2 * (3 * S * 2) + i * 4160 for i in range(2)]
    probs_off = [arena_base + 24576 + 8320 + i * (S * 2) for i in range(2)]
    asb_off = arena_base + 24576 + 8320 + 8192
    attn_arena_end = asb_off + S * 2
    h_off = arena_base
    psb_off = arena_base + FCH * 1024 * 2
    ffn_arena_end = psb_off + 2 * 1024 * 2
    cursor[0] = ((max(attn_arena_end, ffn_arena_end) + 31) // 32) * 32

    q_sb, k_sb, vt_sb, v_sb, probs_sb = [], [], [], [], []
    for i in range(2):
        t, _ = alloc(f"q{i}", (128, S), BF16, at=qkv_off[i]); q_sb.append(t)
        t, _ = alloc(f"k{i}", (128, S), BF16, at=qkv_off[i] + S * 2); k_sb.append(t)
        t, _ = alloc(f"vt{i}", (128, S), BF16, at=qkv_off[i] + 2 * S * 2); vt_sb.append(t)
        t, _ = alloc(f"v{i}", (128, 2 * 16 * 65), BF16, at=v_off[i]); v_sb.append(t)
        t, _ = alloc(f"probs{i}", (128, S), BF16, at=probs_off[i]); probs_sb.append(t)
    asb, _ = alloc("asb", (128, S), BF16, at=asb_off)
    h_sb, _ = alloc("h", (128, FCH * 1024), BF16, at=h_off)
    p_sb, _ = alloc("psb", (128, 2 * 1024), BF16, at=psb_off)

    RING = 32
    ring_sb, _ = alloc("ring", (128, RING * 128), BF16)
    rows, rows_off = alloc("rows", (65, S), F32)
    rows_r, _ = alloc("rows_r", (65, S), F32R, at=rows_off)
    scr, scr_off = alloc("scr", (128, S), F32)
    scr_r, _ = alloc("scr_r", (128, S), F32R, at=scr_off)
    scr2, scr2_off = alloc("scr2", (128, 512), F32)
    vrow, _ = alloc("vrow", (1, 512), F32, at=scr2_off)
    vrow2, _ = alloc("vrow2", (1, 512), F32, at=scr_off + 4096)
    b3_sb, _ = alloc("b3", (65, D), BF16)
    bm_sb, _ = alloc("bm", (1, D), BF16)
    row0p = []
    for i in range(2):
        t, _ = alloc(f"row0{i}", (1, 1024), BF16, at=asb_off + 2048 * i)
        row0p.append(t)
    wb2 = []
    negB, _ = alloc("negB", (128, 4), F32)
    la_sb, _ = alloc("la", (128, 16 * 8), F32)
    sink, _ = alloc("sink", (128, 64), F32)
    wbt_sb, _ = alloc("wbt", (128, 16 * 8), BF16)
    mcols, _ = alloc("mcols", (128, 32), F32)
    for i in range(2):
        t, _ = alloc(f"wb{i}", (128, 1024), BF16, at=arena_base + 36864 + 2048 * i)
        wb2.append(t)
    rw_sb, rw_off = alloc("rw", (128, 64), F32)
    rw_r, _ = alloc("rw_r", (128, 64), F32R, at=rw_off)
    ident, _ = alloc("ident", (128, 128), BF16)
    caus, _ = alloc("caus", (128, 128), BF16)
    ones3, _ = alloc("ones3", (65, 512), BF16)
    ones65, o65_off = alloc("ones65", (65, 128), F32)
    ones65_r, _ = alloc("ones65_r", (65, 128), F32R, at=o65_off)
    onecol, oc_off = alloc("onecol", (128, 1), F32)
    onecol_r, _ = alloc("onecol_r", (128, 1), F32R, at=oc_off)
    tiny, _ = alloc("tiny", (1, 16), F32)
    tiny2, _ = alloc("tiny2", (1, 24), F32)
    rbt_sb, _ = alloc("rbt", (128, 8), F32)
    cs_sb, _ = alloc("cs", (128, 8), F32)
    assert cursor[0] <= nc.sbuf_top, (cursor[0], nc.sbuf_top)

    pb = [nc.ctx.enter_context(nc.psum_tensor(f"pb{i}", [128, 512], F32))
          for i in range(6)]
    pT = nc.ctx.enter_context(nc.psum_tensor("pT", [128, 1024], BF16))
    pS = nc.ctx.enter_context(nc.psum_tensor("pS", [128, 512], F32))
    B = [Res() for _ in range(6)]
    BT = Res()
    BS = Res()

    p.ctx = nc.ctx
    s_t = p.sem("s_t")
    s_s = p.sem("s_s")
    s_v = p.sem("s_v")
    s_g = p.sem("s_g")
    s_w = p.sem("s_w")
    s_x = p.sem("s_x")
    s_y = p.sem("s_y")
    MARKS = {"tensor": s_t, "scalar": s_s, "vector": s_v, "gpsimd": s_g}

    def M(eng):
        return (MARKS[eng], 1)

    def mm(out, lhsT, rhs, start, stop, mark=False):
        return p.op("tensor",
                    lambda t, o=out, a=lhsT, b=rhs, st=start, sp=stop:
                    t.matmul(o, a, b, start=st, stop=sp),
                    inc=M("tensor") if mark else None)

    def tr(out, in_, mark=False):
        return p.op("tensor",
                    lambda t, o=out, a=in_: t.transpose(o, a, ident[:, :]),
                    inc=M("tensor") if mark else None)

    def sc(fn, mark=False):
        return p.op("scalar", fn, inc=M("scalar") if mark else None)

    def ve(fn, mark=False):
        return p.op("vector", fn, inc=M("vector") if mark else None)

    ring_pos = [0]
    ring_chunks = []
    ring_dmas = [0]
    ring_freed = [0]
    free_marks = {}

    def ring_load(src_ap, n_tiles):
        start = ring_pos[0]
        slots = []
        segs = []
        t0 = start
        rem = n_tiles
        src_i = 0
        while rem > 0:
            slot = t0 % RING
            seg = min(rem, RING - slot)
            segs.append((slot, src_i, seg))
            t0 += seg
            src_i += seg
            rem -= seg
        last_tile = start + n_tiles - 1
        need_freed_tile = last_tile - RING
        if need_freed_tile >= 0:
            for ci, (cs, ce, _) in enumerate(ring_chunks):
                if cs <= need_freed_tile < ce:
                    wantc = ci + 1
                    break
            else:
                wantc = 0
            for ci in range(ring_freed[0], wantc):
                p.wait("gpsimd", free_marks[ci])
            ring_freed[0] = max(ring_freed[0], wantc)
        for slot, si, seg in segs:
            dst = ring_sb[:, slot * 128:(slot + seg) * 128].rearrange(
                "p (n f) -> p n f", f=128)
            sap = src_ap[si:si + seg, :, :].rearrange("n p f -> p n f")
            p.op("gpsimd",
                 lambda g, d=dst, s=sap: g.dma_start(out=d, in_=s),
                 inc=(s_w, 16))
            ring_dmas[0] += 1
        ring_chunks.append((start, start + n_tiles, ring_dmas[0]))
        ring_pos[0] = start + n_tiles
        wait_mark = (s_w, 16 * ring_dmas[0])
        chunk_idx = len(ring_chunks) - 1
        for i in range(n_tiles):
            slots.append(ring_sb[:, ((start + i) % RING) * 128:((start + i) % RING) * 128 + 128])
        return slots, wait_mark, chunk_idx

    def ring_free(chunk_idx, mark):
        free_marks[chunk_idx] = mark

    def gdma(dst, src):
        p.op("gpsimd", lambda g, d=dst, s=src: g.dma_start(out=d, in_=s),
             inc=(s_g, 16))

    gdma(ident[:, :], identv[:, :])
    gdma(caus[:, :], maskv[:, :])
    gdma(b3_sb[0:1, :], bias3[0:1, :])
    gdma(b3_sb[32:33, :], bias3[1:2, :])
    gdma(b3_sb[64:65, :], bias3[2:3, :])
    gdma(bm_sb[0:1, :], bmv[0:1, :])
    for d in range(DCH):
        gdma(rw_sb[:, 8 * d:8 * d + 8], rwt[d, :, :])
    gdma(sink[:, 9:10], tempv[:, :])
    gdma(sink[:, 0:8], rbv[:, :])
    consts_g = p.cur(s_g)
    p.wait("vector", consts_g)
    for bb in (0, 32, 64):
        ve(lambda v, bb=bb: v.memset(ones3[bb:bb + 1, :], 1.0))
        ve(lambda v, bb=bb: v.memset(ones65[bb:bb + 1, :], 1.0))
    ve(lambda v: v.memset(onecol[:, :], 1.0))
    for i in range(2):
        for base in (0, 16 * 65):
            for tt in range(16):
                mk = ve(lambda v, i=i, base=base, tt=tt:
                        v.memset(v_sb[i][:, base + 65 * tt + 64:base + 65 * tt + 65],
                                 1.0), mark=True)
    consts_v = p.cur(s_v)

    ve(lambda v: v.tensor_scalar_max(sink[:, 10:11], sink[:, 9:10], 0.1))
    ve(lambda v: v.engine_nop())
    ve(lambda v: v.engine_nop())
    ve(lambda v: v.reciprocal(sink[:, 8:9], sink[:, 10:11]))
    ve(lambda v: v.engine_nop())
    ve(lambda v: v.engine_nop())
    mk = ve(lambda v: v.tensor_scalar(rbt_sb[:, 0:8], sink[:, 0:8], sink[:, 8:9],
                                      0.0, ALU.mult, ALU.add), mark=True)
    prep_v = p.cur(s_v)
    rbt_ready = p.cur(s_v)
    B[2].acquire(p, "tensor")
    p.wait("tensor", consts_g)
    p.wait("tensor", consts_v)
    for d in range(DCH):
        mk = mm(pb[2][0:1, 0:8], onecol_r[:, :], rw_r[:, 8 * d:8 * d + 8],
                d == 0, d == DCH - 1, mark=(d == DCH - 1))
    B[2].produce(mk)
    B[2].consume_wait(p, "scalar")
    mk = sc(lambda s: s.copy(tiny2[0:1, 8:16], pb[2][0:1, 0:8]), mark=True)
    B[2].used(mk)
    cs_row_ready = p.cur(s_s)
    B[3].acquire(p, "tensor")
    p.wait("tensor", cs_row_ready)
    mk = mm(pb[3][:, 0:8], ones65[0:1, :], tiny2[0:1, 8:16], True, True, mark=True)
    B[3].produce(mk)
    B[3].consume_wait(p, "scalar")
    mk = sc(lambda s: s.copy(cs_sb[:, 0:8], pb[3][:, 0:8]), mark=True)
    B[3].used(mk)
    cs_ready = p.cur(s_s)

    def xap(d, lo, hi, r=False):
        t = x_r if r else x_sb
        return t[:, S * d + lo:S * d + hi]

    def lap(d, lo, hi):
        return lnT[:, S * d + lo:S * d + hi]

    R_qkv = [Res(), Res()]
    R_v = [Res(), Res()]
    R_probs = [Res(), Res()]
    R_asb = Res()
    R_lnT = Res()
    R_attnT = Res()
    R_arena = Res()
    R_psb = [Res(), Res()]
    R_h = Res()
    wb_res = [Res(), Res()]
    row0_res = [Res(), Res()]
    R_la = Res()
    R_rows = Res()
    R_scr = [Res(), Res()]
    R_x = Res()

    x_dmas = [0]
    y_dmas = [0]
    ystore_cnt = {}
    tapn = [0]

    for b in range(n_batch):
        for d in range(DCH):
            if b > 0:
                p.wait("sync", (s_y, 16 * ystore_cnt[(b - 1, 1, d)]))
            p.op("sync", lambda s, b=b, d=d: s.dma_start(out=xap(d, 0, S), in_=xT[b, d, :, :]),
                 inc=(s_x, 16))
            x_dmas[0] += 1
        x_ready = (s_x, 16 * x_dmas[0])
        if b == TAPB and "xload" in tap_dram:
            p.wait("gpsimd", x_ready)
            p.op("gpsimd", lambda g: g.dma_start(out=tap_dram["xload"][:, :],
                                                 in_=x_sb[:, :]), inc=(s_y, 16))
            y_dmas[0] += 1

        _ln1_s_marks = {}
        for si in range(TS):
            lo, hi = 512 * si, 512 * si + 512
            for d in range(DCH):
                par = d % 2
                R_scr[par].acquire(p, "vector")
                p.wait("vector", x_ready)
                mk = ve(lambda v, d=d, lo=lo, hi=hi, par=par:
                        v.tensor_mul(scr[:, 512 * par:512 * par + 512],
                                     xap(d, lo, hi), xap(d, lo, hi)), mark=True)
                R_scr[par].produce(mk)
                if d == 0:
                    BS.acquire(p, "tensor")
                    B[5].acquire(p, "tensor")
                p.wait("tensor", x_ready)
                mm(pS[0:1, 0:512], onecol_r[:, :], xap(d, lo, hi, r=True),
                   d == 0, d == DCH - 1, mark=(d == DCH - 1))
                R_scr[par].consume_wait(p, "tensor")
                mk = mm(pb[5][0:1, 0:512], onecol_r[:, :],
                        scr_r[:, 512 * par:512 * par + 512],
                        d == 0, d == DCH - 1, mark=True)
                R_scr[par].used(mk)
            BS.produce(p.cur(s_t))
            B[5].produce(mk)
            R_rows.acquire(p, "scalar")
            BS.consume_wait(p, "scalar")
            mk = sc(lambda s, lo=lo: s.mul(rows[0:1, lo:lo + 512], pS[0:1, 0:512],
                                           1.0 / D), mark=True)
            BS.used(mk)
            m_ready = p.cur(s_s)
            B[5].consume_wait(p, "vector")
            p.wait("vector", m_ready)
            ve(lambda v, lo=lo: v.tensor_mul(vrow[0:1, :], rows[0:1, lo:lo + 512],
                                             rows[0:1, lo:lo + 512]))
            mk = ve(lambda v: v.tensor_scalar(vrow2[0:1, :], pb[5][0:1, 0:512],
                                              1.0 / D, 0.0, ALU.mult, ALU.add), mark=True)
            B[5].used(mk)
            mk = ve(lambda v: v.tensor_sub(vrow[0:1, :], vrow2[0:1, :], vrow[0:1, :]),
                    mark=True)
            var_ready = p.cur(s_v)
            p.wait("scalar", var_ready)
            sc(lambda s: s.activation(vrow2[0:1, :], vrow[0:1, :], AF.Ln))
            mk = sc(lambda s, lo=lo: s.activation(rows[32:33, lo:lo + 512],
                                                  vrow2[0:1, :], AF.Exp, scale=-0.5),
                    mark=True)
            R_rows.produce(mk)
            B[3].acquire(p, "tensor")
            B[4].acquire(p, "tensor")
            R_rows.consume_wait(p, "tensor")
            mm(pb[3][:, 0:512], ones65_r[0:1, :], rows_r[0:1, lo:hi], True, True)
            mk = mm(pb[4][:, 0:512], ones65_r[32:33, :], rows_r[32:33, lo:hi],
                    True, True, mark=True)
            B[3].produce(mk)
            B[4].produce(mk)
            R_rows.used(mk)
            B[3].consume_wait(p, "vector")
            for d in range(DCH):
                par = d % 2
                R_scr[par].acquire(p, "vector")
                mk = ve(lambda v, d=d, lo=lo, hi=hi, par=par:
                        v.tensor_sub(scr[:, 512 * par:512 * par + 512],
                                     xap(d, lo, hi), pb[3][:, 0:512]), mark=True)
                R_scr[par].produce(mk)
                mk = ve(lambda v, d=d, lo=lo, hi=hi, par=par:
                        v.tensor_mul(lap(d, lo, hi),
                                     scr[:, 512 * par:512 * par + 512],
                                     pb[4][:, 0:512]), mark=True)
                R_scr[par].used(mk)
                B[4].used(mk)
                B[3].used(mk)
            _ln1_s_marks[si] = p.cur(s_v)
        R_lnT.produce(p.cur(s_v))
        ln1_done_v = p.cur(s_v)
        ln1_s_marks = dict(_ln1_s_marks)

        if "attn" in phases:
            for i in range(2):
                R_v[i].acquire(p, "vector")
                for base in (0, 16 * 65):
                    for tt in range(16):
                        mk = ve(lambda v, i=i, base=base, tt=tt:
                                v.memset(v_sb[i][:, base + 65 * tt + 64:
                                                 base + 65 * tt + 65], 1.0),
                                mark=(tt == 15))
                R_v[i].produce(mk)
            vones_v = p.cur(s_v)
            pair_info = {}

            def prep_pair_chunks(pr):
                par = pr % 2
                chunks = []
                slots, wmark, ck = ring_load(wqkv[pr, 0], DCH)
                slots_k, wmark_k, ck_k = ring_load(wqkv[pr, 1], DCH)
                slots_v, wmark_v, ck_v = ring_load(wqkv[pr, 2], DCH)

                def c_acq():
                    R_qkv[par].acquire(p, "scalar")
                chunks.append(c_acq)

                def proj_chunk(kind, wslots, bias_row, bank, si, wm, ckx, pr=pr,
                               par=par):
                    def go():
                        dst = {"q": q_sb[par], "k": k_sb[par], "v": vt_sb[par]}[kind]
                        p.wait("tensor", wm)
                        p.wait("tensor", ln1_s_marks[si])
                        B[bank].acquire(p, "tensor")
                        for d in range(DCH):
                            mm(pb[bank][:, 0:512], wslots[d],
                               lap(d, 512 * si, 512 * si + 512), d == 0, False)
                        mk = mm(pb[bank][:, 0:512],
                                b3_sb[bias_row:bias_row + 1, 128 * pr:128 * pr + 128],
                                ones3[bias_row:bias_row + 1, :], False, True,
                                mark=True)
                        B[bank].produce(mk)
                        B[bank].consume_wait(p, "scalar")
                        mk = sc(lambda s, dst=dst, si=si, bank=bank:
                                s.copy(dst[:, 512 * si:512 * si + 512],
                                       pb[bank][:, 0:512]), mark=True)
                        B[bank].used(mk)
                        if si == TS - 1:
                            ring_free(ckx, p.cur(s_t))
                    return go

                for si in range(TS):
                    chunks.append(proj_chunk("q", slots, 0, (0, 1)[si % 2], si,
                                             wmark, ck))
                for si in range(TS):
                    chunks.append(proj_chunk("k", slots_k, 32, (2, 3)[si % 2], si,
                                             wmark_k, ck_k))
                for si in range(TS):
                    chunks.append(proj_chunk("v", slots_v, 64, (4, 5)[si % 2], si,
                                             wmark_v, ck_v))

                def c_qkvdone():
                    pair_info[pr] = {"qkv": p.cur(s_s)}
                    R_qkv[par].produce(pair_info[pr]["qkv"])
                chunks.append(c_qkvdone)

                def vt_chunk(t0, par=par, pr=pr):
                    def go():
                        if t0 == 0:
                            R_v[par].acquire(p, "scalar")
                            p.wait("scalar", vones_v)
                        for tt in range(t0, t0 + 4):
                            BT.acquire(p, "tensor")
                            p.wait("tensor", pair_info[pr]["qkv"])
                            mk = tr(pT[:, 0:128],
                                    vt_sb[par][:, 128 * tt:128 * tt + 128], mark=True)
                            BT.produce(mk)
                            BT.consume_wait(p, "scalar")
                            sc(lambda s, par=par, tt=tt:
                               s.copy(v_sb[par][:, 65 * tt:65 * tt + 64], pT[:, 0:64]),
                               mark=True)
                            sc(lambda s, par=par, tt=tt:
                               s.copy(v_sb[par][:, 16 * 65 + 65 * tt:
                                                16 * 65 + 65 * tt + 64],
                                      pT[:, 64:128]), mark=True)
                            BT.used(p.cur(s_s))
                        if t0 == 12:
                            R_v[par].produce(p.cur(s_s))
                    return go

                for t0 in range(0, 16, 4):
                    chunks.append(vt_chunk(t0))

                def c_bound1(par=par, pr=pr):
                    R_qkv[par].consume_wait(p, "vector")
                    R_scr[0].acquire(p, "vector")
                    R_scr[1].acquire(p, "vector")
                    ve(lambda v, par=par: v.tensor_mul(scr[:, 0:S], q_sb[par][:, :],
                                                       q_sb[par][:, :]))
                    ve(lambda v: v.reduce_max(scr2[:, 0:1], scr[:, 0:S], axis=AX.X),
                       mark=True)
                    ve(lambda v, par=par: v.tensor_mul(scr[:, 0:S], k_sb[par][:, :],
                                                       k_sb[par][:, :]))
                    mk2 = ve(lambda v: v.reduce_max(scr2[:, 1:2], scr[:, 0:S],
                                                    axis=AX.X), mark=True)
                    R_qkv[par].used(mk2)
                    R_scr[0].produce(mk2)
                    R_scr[1].produce(mk2)
                chunks.append(c_bound1)

                def c_bound2(par=par, pr=pr):
                    qmx_ready = p.cur(s_v)
                    BS.acquire(p, "tensor")
                    p.wait("tensor", qmx_ready)
                    mm(pS[0:1, 0:1], onecol[0:64, :], scr2[0:64, 0:1], True, True)
                    mm(pS[0:1, 1:2], onecol[64:128, :], scr2[64:128, 0:1], True, True)
                    mm(pS[0:1, 2:3], onecol[0:64, :], scr2[0:64, 1:2], True, True)
                    mk = mm(pS[0:1, 3:4], onecol[64:128, :], scr2[64:128, 1:2],
                            True, True, mark=True)
                    BS.produce(mk)
                    BS.consume_wait(p, "scalar")
                    mk = sc(lambda s: s.copy(tiny[0:1, 4:8], pS[0:1, 0:4]), mark=True)
                    BS.used(mk)
                    p.wait("vector", p.cur(s_s))
                    ve(lambda v: v.tensor_mul(tiny[0:1, 8:9], tiny[0:1, 4:5],
                                              tiny[0:1, 6:7]))
                    ve(lambda v: v.tensor_mul(tiny[0:1, 9:10], tiny[0:1, 5:6],
                                              tiny[0:1, 7:8]), mark=True)
                chunks.append(c_bound2)

                def c_bound3(par=par, pr=pr):
                    p.wait("scalar", p.cur(s_v))
                    sc(lambda s: s.activation(tiny[0:1, 10:12], tiny[0:1, 8:10],
                                              AF.Ln))
                    sc(lambda s: s.copy(tiny[0:1, 4:6], tiny[0:1, 4:6]))
                    sc(lambda s: s.copy(tiny[0:1, 4:6], tiny[0:1, 4:6]))
                    sc(lambda s: s.activation(tiny[0:1, 12:14], tiny[0:1, 10:12],
                                              AF.Exp, scale=0.5))
                    sc(lambda s: s.copy(tiny[0:1, 4:6], tiny[0:1, 4:6]))
                    sc(lambda s: s.copy(tiny[0:1, 4:6], tiny[0:1, 4:6]))
                    sc(lambda s: s.mul(tiny[0:1, 14:16], tiny[0:1, 12:14], -0.125),
                       mark=True)
                    negb_row = p.cur(s_s)
                    BS.acquire(p, "tensor")
                    p.wait("tensor", negb_row)
                    mk = mm(pS[:, 0:2], ones65[0:1, :], tiny[0:1, 14:16], True, True,
                            mark=True)
                    BS.produce(mk)
                    BS.consume_wait(p, "scalar")
                    mk = sc(lambda s, par=par: s.copy(negB[:, 2 * par:2 * par + 2],
                                                      pS[:, 0:2]), mark=True)
                    BS.used(mk)
                    sc(lambda s: s.copy(tiny[0:1, 4:6], tiny[0:1, 4:6]))
                    sc(lambda s: s.copy(tiny[0:1, 4:6], tiny[0:1, 4:6]))
                    pair_info[pr]["negb"] = p.cur(s_s)
                chunks.append(c_bound3)
                return chunks

            def attention_pair(pr, feed):
                par = pr % 2
                qkv_ready_s = pair_info[pr]["qkv"]
                negb_ready = pair_info[pr]["negb"]
                for hh in range(2):
                    hb = 64 * hh
                    vcolbase = 16 * 65 * hh

                    def S_emit(qi, hh=hh, hb=hb, par=par):
                        ppar = qi % 2
                        nk = qi + 1
                        R_probs[ppar].acquire(p, "scalar")
                        for g0 in range(0, nk, 4):
                            gn = min(4, nk - g0)
                            bank = (0, 1, 4)[sgrp[0] % 3]
                            sgrp[0] += 1
                            B[bank].acquire(p, "tensor")
                            p.wait("tensor", qkv_ready_s)
                            for ki in range(gn):
                                kc = g0 + ki
                                diag = (kc == qi)
                                mm(pb[bank][:, 128 * ki:128 * ki + 128],
                                   k_sb[par][hb:hb + 64, 128 * kc:128 * kc + 128],
                                   q_sb[par][hb:hb + 64, 128 * qi:128 * qi + 128],
                                   True, not diag,
                                   mark=(ki == gn - 1 and not diag))
                                if diag:
                                    mm(pb[bank][:, 128 * ki:128 * ki + 128],
                                       ident[:, :], caus[:, :], False, True,
                                       mark=(ki == gn - 1))
                            B[bank].produce(p.cur(s_t))
                            B[bank].consume_wait(p, "scalar")
                            p.wait("scalar", negb_ready)
                            mk = sc(lambda s, ppar=ppar, g0=g0, gn=gn, bank=bank,
                                    hh=hh, par=par:
                                    s.activation(
                                        probs_sb[ppar][:, 128 * g0:128 * g0 + 128 * gn],
                                        pb[bank][:, 0:128 * gn], AF.Exp,
                                        bias=negB[:, 2 * par + hh:2 * par + hh + 1],
                                        scale=0.125),
                                    mark=True)
                            B[bank].used(mk)
                        R_probs[ppar].produce(p.cur(s_s))

                    def PV_emit(qi, hh=hh, par=par, vcolbase=vcolbase):
                        ppar = qi % 2
                        nk = qi + 1
                        bank = 2 + hh
                        B[bank].acquire(p, "tensor")
                        R_probs[ppar].consume_wait(p, "tensor")
                        R_v[par].consume_wait(p, "tensor")
                        for kc in range(nk):
                            mk = mm(pb[bank][:, 0:65],
                                    probs_sb[ppar][:, 128 * kc:128 * kc + 128],
                                    v_sb[par][:, vcolbase + 65 * kc:vcolbase + 65 * kc + 65],
                                    kc == 0, kc == nk - 1, mark=(kc == nk - 1))
                        B[bank].produce(mk)
                        R_probs[ppar].used(mk)
                        R_v[par].used(mk)
                        B[bank].consume_wait(p, "vector")
                        rcol = 2 + 2 * hh + (qi % 2)
                        prevm = rl_slots.get(rcol)
                        if prevm is not None:
                            p.wait("vector", prevm)
                        ve(lambda v, bank=bank, rcol=rcol:
                           v.reciprocal(scr2[:, rcol:rcol + 1], pb[bank][:, 64:65]),
                           mark=True)
                        R_asb.acquire(p, "scalar") if (hh == 0 and qi == 0) else None
                        p.wait("scalar", p.cur(s_v))
                        mk = sc(lambda s, bank=bank, qi=qi, hh=hh, rcol=rcol:
                                s.activation(asb[:, 128 * qi + 64 * hh:
                                                 128 * qi + 64 * hh + 64],
                                             pb[bank][:, 0:64], AF.Copy,
                                             scale=scr2[:, rcol:rcol + 1]), mark=True)
                        B[bank].used(mk)
                        rl_slots[rcol] = mk

                    sgrp = [0]
                    rl_slots = {}
                    S_emit(0)
                    feed()
                    for qi in range(1, NQT):
                        S_emit(qi)
                        feed()
                        PV_emit(qi - 1)
                        feed()
                    PV_emit(NQT - 1)
                    feed()

            pending = prep_pair_chunks(0)
            while pending:
                pending.pop(0)()
            for pr in range(NP):
                par = pr % 2
                pending = prep_pair_chunks(pr + 1) if pr + 1 < NP else []

                def feed():
                    if pending:
                        pending.pop(0)()
                attention_pair(pr, feed)
                while pending:
                    pending.pop(0)()
                asb_ready = p.cur(s_s)
                R_asb.produce(asb_ready)
                for tt in range(16):
                    BT.acquire(p, "tensor")
                    p.wait("tensor", asb_ready)
                    mk = tr(pT[:, 0:128], asb[:, 128 * tt:128 * tt + 128], mark=True)
                    BT.produce(mk)
                    R_asb.used(mk)
                    BT.consume_wait(p, "scalar")
                    mk = sc(lambda s, pr=pr, tt=tt:
                            s.copy(attnT[:, S * pr + 128 * tt:S * pr + 128 * tt + 128],
                                   pT[:, 0:128]), mark=True)
                    BT.used(mk)
            R_attnT.produce(p.cur(s_s))

            for dm in range(DCH):
                slots, wmark, ck = ring_load(wm[dm], DCH)
                for si in range(TS):
                    bank = si % 2
                    B[bank].acquire(p, "tensor")
                    p.wait("tensor", wmark)
                    R_attnT.consume_wait(p, "tensor")
                    for d in range(DCH):
                        mm(pb[bank][:, 0:512], slots[d],
                           attnT[:, S * d + 512 * si:S * d + 512 * si + 512],
                           d == 0, False)
                    mk = mm(pb[bank][:, 0:512], bm_sb[0:1, 128 * dm:128 * dm + 128],
                            ones3[0:1, :], False, True, mark=True)
                    B[bank].produce(mk)
                    if si == TS - 1:
                        R_attnT.used(mk)
                        ring_free(ck, p.cur(s_t))
                    B[bank].consume_wait(p, "vector")
                    mk = ve(lambda v, dm=dm, si=si, bank=bank:
                            v.tensor_add(xap(dm, 512 * si, 512 * si + 512),
                                         xap(dm, 512 * si, 512 * si + 512),
                                         pb[bank][:, 0:512]), mark=True)
                    B[bank].used(mk)
            wm_done_v = p.cur(s_v)

        if "ffn" not in phases:
            xdone = p.cur(s_v)
            for d in range(DCH):
                p.wait("sync", xdone)
                p.op("sync",
                     lambda s, b=b, d=d: s.dma_start(out=y[b, d, :, :], in_=xap(d, 0, S)),
                     inc=(s_y, 16))
                y_dmas[0] += 1
            continue

        for si in range(TS):
            lo, hi = 512 * si, 512 * si + 512
            for d in range(DCH):
                par = d % 2
                R_scr[par].acquire(p, "vector")
                mk = ve(lambda v, d=d, lo=lo, hi=hi, par=par:
                        v.tensor_mul(scr[:, 512 * par:512 * par + 512],
                                     xap(d, lo, hi), xap(d, lo, hi)), mark=True)
                R_scr[par].produce(mk)
                if d == 0:
                    BS.acquire(p, "tensor")
                    B[5].acquire(p, "tensor")
                    if "attn" in phases:
                        p.wait("tensor", wm_done_v)
                mm(pS[0:1, 0:512], onecol_r[:, :], xap(d, lo, hi, r=True),
                   d == 0, d == DCH - 1, mark=(d == DCH - 1))
                R_scr[par].consume_wait(p, "tensor")
                mk = mm(pb[5][0:1, 0:512], onecol_r[:, :],
                        scr_r[:, 512 * par:512 * par + 512],
                        d == 0, d == DCH - 1, mark=True)
                R_scr[par].used(mk)
            BS.produce(p.cur(s_t))
            B[5].produce(mk)
            R_rows.acquire(p, "scalar")
            BS.consume_wait(p, "scalar")
            mk = sc(lambda s, lo=lo: s.mul(rows[0:1, lo:lo + 512], pS[0:1, 0:512],
                                           1.0 / D), mark=True)
            BS.used(mk)
            m_ready = p.cur(s_s)
            B[5].consume_wait(p, "vector")
            p.wait("vector", m_ready)
            ve(lambda v, lo=lo: v.tensor_mul(vrow[0:1, :], rows[0:1, lo:lo + 512],
                                             rows[0:1, lo:lo + 512]))
            mk = ve(lambda v: v.tensor_scalar(vrow2[0:1, :], pb[5][0:1, 0:512],
                                              1.0 / D, 0.0, ALU.mult, ALU.add), mark=True)
            B[5].used(mk)
            ve(lambda v: v.scalar_tensor_tensor(vrow[0:1, :], vrow2[0:1, :], EPS,
                                                vrow[0:1, :], ALU.add, ALU.subtract),
               mark=True)
            var_ready = p.cur(s_v)
            p.wait("scalar", var_ready)
            sc(lambda s: s.activation(vrow2[0:1, :], vrow[0:1, :], AF.Ln))
            mk = sc(lambda s, lo=lo: s.activation(rows[32:33, lo:lo + 512],
                                                  vrow2[0:1, :], AF.Exp, scale=-0.5),
                    mark=True)
            R_rows.produce(mk)
        ln2_stats_s = p.cur(s_s)

        BS.acquire(p, "tensor")
        p.wait("tensor", ln2_stats_s)
        for tt in range(16):
            mm(pS[:, tt:tt + 1], rows[0:1, 128 * tt:128 * tt + 128],
               onecol[0:1, 0:1], True, True)
        for tt in range(16):
            mk = mm(pS[:, 16 + tt:17 + tt], rows[32:33, 128 * tt:128 * tt + 128],
                    onecol[32:33, 0:1], True, True, mark=(tt == 15))
        BS.produce(mk)
        BS.consume_wait(p, "scalar")
        mk = sc(lambda s: s.copy(mcols[:, 0:16], pS[:, 0:16]), mark=True)
        mk = sc(lambda s: s.copy(scr2[:, 8:24], pS[:, 16:32]), mark=True)
        BS.used(mk)
        p.wait("vector", p.cur(s_s))
        p.wait("vector", rbt_ready)
        mk = ve(lambda v: v.tensor_scalar(mcols[:, 16:32], scr2[:, 8:24],
                                          sink[:, 8:9], -1.0, ALU.mult, ALU.mult),
                mark=True)
        cols_ready = p.cur(s_v)
        R_la.acquire(p, "vector")
        R_scr[0].acquire(p, "scalar")
        R_scr[1].acquire(p, "scalar")
        for tt in range(16):
            bank = tt % 2
            B[bank].acquire(p, "tensor")
            for d in range(DCH):
                mk = mm(pb[bank][:, 0:8], xap(d, 128 * tt, 128 * tt + 128, r=True),
                        rw_r[:, 8 * d:8 * d + 8], d == 0, d == DCH - 1,
                        mark=(d == DCH - 1))
            B[bank].produce(mk)
            B[bank].consume_wait(p, "scalar")
            mk = sc(lambda s, tt=tt, bank=bank:
                    s.copy(scr[:, 256 + 8 * tt:256 + 8 * tt + 8], pb[bank][:, 0:8]),
                    mark=True)
            B[bank].used(mk)
        raw_s = p.cur(s_s)
        p.wait("vector", raw_s)
        p.wait("vector", cols_ready)
        p.wait("vector", cs_ready)
        p.wait("vector", rbt_ready)
        for tt in range(16):
            ve(lambda v, tt=tt: v.scalar_tensor_tensor(
                scr[:, 384 + 8 * tt:384 + 8 * tt + 8], cs_sb[:, 0:8],
                mcols[:, tt:tt + 1], scr[:, 256 + 8 * tt:256 + 8 * tt + 8],
                ALU.mult, ALU.subtract))
        for tt in range(16):
            ve(lambda v, tt=tt: v.tensor_scalar(
                scr[:, 512 + 8 * tt:512 + 8 * tt + 8],
                scr[:, 384 + 8 * tt:384 + 8 * tt + 8],
                mcols[:, 16 + tt:17 + tt], 0.0, ALU.mult, ALU.add))
        for tt in range(16):
            mk = ve(lambda v, tt=tt: v.tensor_add(
                la_sb[:, 8 * tt:8 * tt + 8], scr[:, 512 + 8 * tt:512 + 8 * tt + 8],
                rbt_sb[:, 0:8]), mark=(tt == 15))
        R_scr[0].produce(mk)
        R_scr[1].produce(mk)
        R_la.produce(p.cur(s_v))
        if b == TAPB and "la0" in tap_dram:
            p.wait("gpsimd", p.cur(s_v))
            p.op("gpsimd", lambda g: g.dma_start(out=tap_dram["la0"][:, :],
                                                 in_=la_sb[:, :]), inc=(s_y, 16))
            y_dmas[0] += 1
        if b == TAPB and "mcols0" in tap_dram:
            p.wait("gpsimd", p.cur(s_v))
            p.op("gpsimd", lambda g: g.dma_start(out=tap_dram["mcols0"][:, :],
                                                 in_=mcols[:, :]), inc=(s_y, 16))
            p.op("gpsimd", lambda g: g.dma_start(out=tap_dram["cs0"][:, :],
                                                 in_=cs_sb[:, :]), inc=(s_y, 16))
            p.op("gpsimd", lambda g: g.dma_start(out=tap_dram["rbt0"][:, :],
                                                 in_=rbt_sb[:, :]), inc=(s_y, 16))
            p.op("gpsimd", lambda g: g.dma_start(out=tap_dram["tiny0"][:, :],
                                                 in_=tiny[0:1, :]), inc=(s_y, 16))
            p.op("gpsimd", lambda g: g.dma_start(out=tap_dram["rtcol0"][:, :],
                                                 in_=sink[:, 8:9]), inc=(s_y, 16))
            y_dmas[0] += 5
        sink_state = {}

        def emit_sinkhorn():
            for it in range(3):
                for tt in range(16):
                    ve(lambda v, tt=tt: v.reduce_max(scr[:, 1664 + tt:1665 + tt],
                                                     la_sb[:, 8 * tt:8 * tt + 8],
                                                     axis=AX.X))
                for tt in range(16):
                    mk = ve(lambda v, tt=tt: v.tensor_scalar_mul(
                        scr[:, 1680 + tt:1681 + tt], scr[:, 1664 + tt:1665 + tt], -1.0),
                        mark=(tt == 15))
                rowmax_v = p.cur(s_v)
                p.wait("scalar", rowmax_v)
                for tt in range(16):
                    sc(lambda s, tt=tt: s.activation(
                        scr[:, 8 * tt:8 * tt + 8], la_sb[:, 8 * tt:8 * tt + 8], AF.Exp,
                        bias=scr[:, 1680 + tt:1681 + tt],
                        accum_out=scr[:, 1728 + tt:1729 + tt]))
                for tt in range(16):
                    mk = sc(lambda s, tt=tt: s.activation(
                        scr[:, 1712 + tt:1713 + tt], scr[:, 1728 + tt:1729 + tt], AF.Ln),
                        mark=(tt == 15))
                lse_s = p.cur(s_s)
                p.wait("vector", lse_s)
                for tt in range(16):
                    mk = ve(lambda v, tt=tt: v.tensor_scalar(
                        la_sb[:, 8 * tt:8 * tt + 8], la_sb[:, 8 * tt:8 * tt + 8],
                        scr[:, 1664 + tt:1665 + tt], scr[:, 1712 + tt:1713 + tt],
                        ALU.subtract, ALU.subtract), mark=(tt == 15))
                rowstep_v = p.cur(s_v)
                p.wait("scalar", rowstep_v)
                for tt in range(16):
                    mk = sc(lambda s, tt=tt:
                            s.activation(scr[:, 8 * tt:8 * tt + 8],
                                         la_sb[:, 8 * tt:8 * tt + 8], AF.Exp),
                            mark=True)
                    if tt == 0:
                        BS.acquire(p, "tensor")
                    p.wait("tensor", p.cur(s_s))
                    mk = mm(pS[0:1, 0:8], onecol[:, :], scr[:, 8 * tt:8 * tt + 8],
                            tt == 0, tt == 15, mark=(tt == 15))
                BS.produce(mk)
                BS.consume_wait(p, "scalar")
                mk = sc(lambda s: s.activation(tiny2[0:1, 16:24], pS[0:1, 0:8], AF.Ln),
                        mark=True)
                BS.used(mk)
                B[4].acquire(p, "tensor")
                p.wait("tensor", p.cur(s_s))
                mk = mm(pb[4][:, 0:8], ones65[0:1, :], tiny2[0:1, 16:24], True, True,
                        mark=True)
                B[4].produce(mk)
                B[4].consume_wait(p, "vector")
                for tt in range(16):
                    mk = ve(lambda v, tt=tt: v.tensor_sub(la_sb[:, 8 * tt:8 * tt + 8],
                                                          la_sb[:, 8 * tt:8 * tt + 8],
                                                          pb[4][:, 0:8]),
                            mark=(tt == 15))
                B[4].used(mk)
            sink_v = p.cur(s_v)
            p.wait("scalar", sink_v)
            for tt in range(16):
                mk = sc(lambda s, tt=tt: s.activation(
                    scr[:, 8 * tt:8 * tt + 8], la_sb[:, 8 * tt:8 * tt + 8], AF.Exp,
                    accum_out=scr[:, 1728 + tt:1729 + tt]), mark=(tt == 15))
            gates_s = p.cur(s_s)
            p.wait("vector", gates_s)
            for tt in range(16):
                ve(lambda v, tt=tt: v.tensor_scalar(scr[:, 1744 + tt:1745 + tt],
                                                    scr[:, 1728 + tt:1729 + tt], 1e-8,
                                                    0.0, ALU.add, ALU.add))
            for tt in range(16):
                ve(lambda v, tt=tt: v.reciprocal(scr[:, 1760 + tt:1761 + tt],
                                                 scr[:, 1744 + tt:1745 + tt]))
            for tt in range(16):
                ve(lambda v, tt=tt: v.tensor_scalar(scr[:, 128 + 8 * tt:128 + 8 * tt + 8],
                                                    scr[:, 8 * tt:8 * tt + 8],
                                                    scr[:, 1760 + tt:1761 + tt], 0.0,
                                                    ALU.mult, ALU.add))
            for tt in range(16):
                ve(lambda v, tt=tt: v.max(out=scr[:, 256 + 8 * tt:256 + 8 * tt + 8],
                                          in_=scr[:, 128 + 8 * tt:128 + 8 * tt + 8]))
            for tt in range(16):
                ve(lambda v, tt=tt: v.tensor_scalar(scr[:, 1664 + tt:1665 + tt],
                                                    scr[:, 256 + 8 * tt:256 + 8 * tt + 1],
                                                    scr[:, 256 + 8 * tt + 1:256 + 8 * tt + 2],
                                                    1e-8, ALU.add, ALU.add))
            for tt in range(16):
                ve(lambda v, tt=tt: v.reciprocal(scr[:, 1680 + tt:1681 + tt],
                                                 scr[:, 1664 + tt:1665 + tt]))
            for tt in range(16):
                ve(lambda v, tt=tt: v.tensor_scalar(scr[:, 384 + 8 * tt:384 + 8 * tt + 8],
                                                    scr[:, 128 + 8 * tt:128 + 8 * tt + 8],
                                                    scr[:, 256 + 8 * tt + 1:256 + 8 * tt + 2],
                                                    1.0, ALU.is_ge, ALU.mult))
            for tt in range(16):
                mk = ve(lambda v, tt=tt: v.scalar_tensor_tensor(
                    scr[:, 512 + 8 * tt:512 + 8 * tt + 8],
                    scr[:, 128 + 8 * tt:128 + 8 * tt + 8],
                    scr[:, 1680 + tt:1681 + tt],
                    scr[:, 384 + 8 * tt:384 + 8 * tt + 8],
                    ALU.mult, ALU.mult), mark=(tt == 15))
            wbv_v = p.cur(s_v)
            p.wait("scalar", wbv_v)
            for tt in range(16):
                mk = sc(lambda s, tt=tt: s.copy(wbt_sb[:, 8 * tt:8 * tt + 8],
                                                scr[:, 512 + 8 * tt:512 + 8 * tt + 8]),
                        mark=(tt == 15))
            wbt_ready = p.cur(s_s)
            for rr in R_scr:
                rr.produce(wbv_v)
                rr.used(wbt_ready)
            sink_state["wbt_ready"] = wbt_ready

        if "ffn" in phases:
            R_arena.acquire(p, "vector")
        for w in range(NW):
            wlo = 1024 * w
            for si in range(WTS):
                lo = wlo + 512 * si
                hi = lo + 512
                B[3].acquire(p, "tensor")
                B[4].acquire(p, "tensor")
                p.wait("tensor", ln2_stats_s)
                mm(pb[3][:, 0:512], ones65_r[0:1, :], rows_r[0:1, lo:hi], True, True)
                mk = mm(pb[4][:, 0:512], ones65_r[32:33, :], rows_r[32:33, lo:hi],
                        True, True, mark=True)
                B[3].produce(mk)
                B[4].produce(mk)
                R_rows.used(mk)
                B[3].consume_wait(p, "vector")
                B[4].consume_wait(p, "vector")
                for d in range(DCH):
                    par = d % 2
                    R_scr[par].acquire(p, "vector")
                    mk = ve(lambda v, d=d, lo=lo, hi=hi, par=par:
                            v.tensor_sub(scr[:, 512 * par:512 * par + 512],
                                         xap(d, lo, hi), pb[3][:, 0:512]), mark=True)
                    R_scr[par].produce(mk)
                    mk = ve(lambda v, d=d, lo=lo, hi=hi, par=par:
                            v.tensor_mul(scr2[:, 0:512],
                                         scr[:, 512 * par:512 * par + 512],
                                         pb[4][:, 0:512]), mark=True)
                    R_scr[par].used(mk)
                    ve(lambda v, d=d, lo=lo, hi=hi:
                       v.tensor_add(xap(d, lo, hi), xap(d, lo, hi), scr2[:, 0:512]))
                    mk = ve(lambda v, d=d, lo=lo, hi=hi:
                            v.tensor_copy(lap(d, lo, hi), scr2[:, 0:512]), mark=True)
                    B[3].used(mk)
                    B[4].used(mk)
            ln2w_v = p.cur(s_v)
            R_lnT.produce(ln2w_v)

            if "ffn" not in phases:
                continue
            R_wb2 = wb_res
            def wb_part1(e):
                par = e % 2
                BS.acquire(p, "tensor")
                p.wait("tensor", sink_state["wbt_ready"])
                rowm = row0_res[par]
                for m2 in rowm.users:
                    p.wait("tensor", m2)
                for j in range(8):
                    tt = 8 * w + j
                    mk = mm(pS[0:1, 128 * (j % 4):128 * (j % 4) + 128],
                            wbt_sb[:, 8 * tt + e:8 * tt + e + 1], ident[:, :],
                            True, True, mark=(j % 4 == 3))
                    if j % 4 == 3:
                        BS.produce(mk)
                        BS.consume_wait(p, "scalar")
                        mk = sc(lambda s, j=j, par=par:
                                s.copy(row0p[par][0:1, 512 * (j // 4):512 * (j // 4) + 512],
                                       pS[0:1, 0:512]), mark=True)
                        BS.used(mk)
                        if j == 3:
                            BS.acquire(p, "tensor")
                rowm.produce(p.cur(s_s))

            def wb_part2(e):
                par = e % 2
                R_wb2[par].acquire(p, "scalar")
                for si in range(WTS):
                    bank = 4 + si
                    B[bank].acquire(p, "tensor")
                    row0_res[par].consume_wait(p, "tensor")
                    mk = mm(pb[bank][:, 0:512], ones3[0:1, 0:128],
                            row0p[par][0:1, 512 * si:512 * si + 512], True, True,
                            mark=True)
                    B[bank].produce(mk)
                    row0_res[par].used(mk)
                    B[bank].consume_wait(p, "scalar")
                    mk = sc(lambda s, si=si, bank=bank, par=par:
                            s.copy(wb2[par][:, 512 * si:512 * si + 512],
                                   pb[bank][:, 0:512]), mark=True)
                    B[bank].used(mk)
                R_wb2[par].produce(p.cur(s_s))

            wo_last_marks = {}
            for e in range(NE):
                wb_sb = wb2[e % 2]
                R_wb = R_wb2[e % 2]
                h_marks = {}
                for f in range(FCH):
                    sg, wg_m, ckg = ring_load(ffn[e, 16 * f:16 * f + 8], 8)
                    sp_, wp_m, ckp = ring_load(ffn[e, 16 * f + 8:16 * f + 16], 8)
                    gbanks = (0, 1) if f % 2 == 0 else (2, 3)
                    p.wait("tensor", wg_m)
                    R_lnT.consume_wait(p, "tensor")
                    for si in range(WTS):
                        B[gbanks[si]].acquire(p, "tensor")
                    for si in range(WTS):
                        for d in range(DCH):
                            mm(pb[gbanks[si]][:, 0:512], sg[d],
                               lap(d, wlo + 512 * si, wlo + 512 * si + 512),
                               d == 0, d == DCH - 1,
                               mark=(d == DCH - 1))
                    gmk = p.cur(s_t)
                    for si in range(WTS):
                        B[gbanks[si]].produce(gmk)
                    ring_free(ckg, gmk)
                    p.wait("tensor", wp_m)
                    pbanks = (4, 5)
                    for si in range(WTS):
                        B[pbanks[si]].acquire(p, "tensor")
                    for si in range(WTS):
                        for d in range(DCH):
                            mm(pb[pbanks[si]][:, 0:512], sp_[d],
                               lap(d, wlo + 512 * si, wlo + 512 * si + 512),
                               d == 0, d == DCH - 1,
                               mark=(d == DCH - 1))
                    pmk = p.cur(s_t)
                    for si in range(WTS):
                        B[pbanks[si]].produce(pmk)
                    ring_free(ckp, pmk)
                    psbpar = f % 2
                    R_psb[psbpar].acquire(p, "scalar")
                    for si in range(WTS):
                        B[pbanks[si]].consume_wait(p, "scalar")
                        mk = sc(lambda s, psbpar=psbpar, si=si, bank=pbanks[si]:
                                s.copy(p_sb[:, 1024 * psbpar + 512 * si:
                                            1024 * psbpar + 512 * si + 512],
                                       pb[bank][:, 0:512]), mark=True)
                        B[pbanks[si]].used(mk)
                    R_psb[psbpar].produce(p.cur(s_s))
                    R_psb[psbpar].consume_wait(p, "vector")
                    for si in range(WTS):
                        B[gbanks[si]].consume_wait(p, "vector")
                        R_h.acquire(p, "vector") if (f == 0 and si == 0) else None
                        mk = ve(lambda v, f=f, si=si, gb=gbanks[si], psbpar=psbpar:
                                v.scalar_tensor_tensor(
                                    h_sb[:, 1024 * f + 512 * si:1024 * f + 512 * si + 512],
                                    pb[gb][:, 0:512], 0.0,
                                    p_sb[:, 1024 * psbpar + 512 * si:1024 * psbpar + 512 * si + 512],
                                    ALU.max, ALU.mult), mark=True)
                        B[gbanks[si]].used(mk)
                        mkpsb = mk
                    h_marks[f] = mkpsb
                    R_psb[psbpar].used(mkpsb)
                R_h.produce(h_marks[FCH - 1])

                if e == 0:
                    if w == 0:
                        emit_sinkhorn()
                    wb_part1(0)
                    wb_part2(0)
                if e + 1 < NE:
                    wb_part1(e + 1)
                for do in range(DCH):
                    s1, m1, c1 = ring_load(ffn[e, 256 + 16 * do:256 + 16 * do + 8], 8)
                    s2, m2, c2 = ring_load(ffn[e, 256 + 16 * do + 8:256 + 16 * do + 16], 8)
                    slots_o, womark = s1 + s2, m2
                    obanks = (0, 1) if do % 2 == 0 else (2, 3)
                    p.wait("tensor", womark)
                    for si in range(WTS):
                        B[obanks[si]].acquire(p, "tensor")
                    for si in range(WTS):
                        for fi in range(FCH):
                            p.wait("tensor", h_marks[fi])
                            mm(pb[obanks[si]][:, 0:512], slots_o[fi],
                               h_sb[:, 1024 * fi + 512 * si:1024 * fi + 512 * si + 512],
                               fi == 0, fi == FCH - 1,
                               mark=(fi == FCH - 1))
                    omk = p.cur(s_t)
                    for si in range(WTS):
                        B[obanks[si]].produce(omk)
                    R_h.used(omk)
                    ring_free(c1, omk)
                    ring_free(c2, omk)
                    for si in range(WTS):
                        B[obanks[si]].consume_wait(p, "vector")
                        R_wb.consume_wait(p, "vector")
                        ve(lambda v, si=si, bank=obanks[si], wb_sb=wb_sb:
                           v.tensor_mul(scr2[:, 0:512], pb[bank][:, 0:512],
                                        wb_sb[:, 512 * si:512 * si + 512]), mark=True)
                        mk = ve(lambda v, do=do, si=si, wlo=wlo:
                                v.tensor_add(xap(do, wlo + 512 * si, wlo + 512 * si + 512),
                                             xap(do, wlo + 512 * si, wlo + 512 * si + 512),
                                             scr2[:, 0:512]), mark=True)
                        B[obanks[si]].used(mk)
                        R_wb.used(mk)
                    if e == NE - 1:
                        wo_last_marks[do] = mk
                if e + 1 < NE:
                    wb_part2(e + 1)
            ffn_w_done = p.cur(s_v)
            for d in range(DCH):
                p.wait("sync", wo_last_marks[d])
                p.op("sync",
                     lambda s, b=b, d=d, wlo=wlo:
                     s.dma_start(out=y[b, d, :, wlo:wlo + 1024],
                                 in_=xap(d, wlo, wlo + 1024)),
                     inc=(s_y, 16))
                y_dmas[0] += 1
                ystore_cnt[(b, w, d)] = y_dmas[0]
        if "ffn" in phases:
            R_arena.produce(p.cur(s_v))

    for tname, shape in taps:
        if tname in ("la0", "mcols0", "cs0", "rbt0", "tiny0", "rtcol0", "xload"):
            continue
        src = {"lnT": lambda: lnT, "attnT": lambda: attnT, "x": lambda: x_sb,
               "la": lambda: la_sb, "wbt": lambda: wbt_sb, "h": lambda: h_sb,
               "q0": lambda: q_sb[0], "k0": lambda: k_sb[0], "v0": lambda: v_sb[0],
               "vt0": lambda: vt_sb[0], "probs0": lambda: probs_sb[0],
               "rows": lambda: rows, "asb": lambda: asb}[tname]()
        p.wait("gpsimd", p.cur(s_v))
        p.wait("gpsimd", p.cur(s_s))
        p.wait("gpsimd", p.cur(s_t))
        pn, fn_ = shape
        p.op("gpsimd", lambda g, d=tap_dram[tname], a=src, pn=pn, fn_=fn_:
             g.dma_start(out=d[:, :], in_=a[0:pn, 0:fn_]), inc=(s_y, 16))
        y_dmas[0] += 1

    p.streams["sync"].append(("w", p.sems[s_y], 16 * y_dmas[0]))

    with nc.Block() as block:
        def run(eng_name):
            def body(eng):
                for item in p.streams[eng_name]:
                    if item[0] == "w":
                        eng.wait_ge(item[1], item[2])
                    else:
                        instr = item[1](eng)
                        if item[2] is not None:
                            instr.then_inc(item[2], item[3])
            return body
        block.tensor(run("tensor"))
        block.scalar(run("scalar"))
        block.vector(run("vector"))
        block.gpsimd(run("gpsimd"))
        block.sync(run("sync"))
    return nc



def _tile_w(w):
    Dd, Ff = w.shape
    return np.ascontiguousarray(
        w.reshape(Dd // 128, 128, Ff // 128, 128).transpose(2, 0, 1, 3))


def prepare_inputs(inputs, n_batch_per_core=NB, n_cores=8):
    L = inputs
    x = np.asarray(L["L0_attn_x"], np.float32)
    wq = _tile_w(np.asarray(L["L0_attn_Wq"], np.float32))
    wk = _tile_w(np.asarray(L["L0_attn_Wk"], np.float32))
    wv = _tile_w(np.asarray(L["L0_attn_Wv"], np.float32))
    wqkv = np.ascontiguousarray(np.stack([wq, wk, wv], axis=1))
    wmt = _tile_w(np.asarray(L["L0_attn_Wm"], np.float32))
    bias3 = np.ascontiguousarray(np.stack([
        np.asarray(L["L0_attn_bq"], np.float32),
        np.asarray(L["L0_attn_bk"], np.float32),
        np.asarray(L["L0_attn_bv"], np.float32)]))
    bmv = np.asarray(L["L0_attn_bm"], np.float32).reshape(1, D)
    wg = np.asarray(L["L0_ffn_wg"], np.float32)
    wp = np.asarray(L["L0_ffn_wp"], np.float32)
    wo = np.asarray(L["L0_ffn_wo"], np.float32)
    ffn = np.empty((NE, 384, 128, 128), np.float32)
    for e in range(NE):
        g_t = _tile_w(wg[e])
        p_t = _tile_w(wp[e])
        o_t = _tile_w(wo[e])
        blk = []
        for f in range(FCH):
            blk.append(g_t[f])
            blk.append(p_t[f])
        for do in range(DCH):
            blk.append(o_t[do])
        ffn[e] = np.concatenate(blk, axis=0)
    rwt = np.ascontiguousarray(
        np.asarray(L["L0_ffn_router_w"], np.float32).T.reshape(DCH, 128, 8))
    rbv = np.ascontiguousarray(np.broadcast_to(
        np.asarray(L["L0_ffn_router_b"], np.float32).reshape(1, 8), (128, 8)))
    tempv = np.ascontiguousarray(np.broadcast_to(
        np.asarray(L["L0_ffn_temperature"], np.float32).reshape(1, 1), (128, 1)))
    identv = np.eye(128, dtype=np.float32)
    kk = np.arange(128)[:, None]
    qq = np.arange(128)[None, :]
    maskv = np.where(kk > qq, -1000.0, 0.0).astype(np.float32)

    shared = dict(wqkv=wqkv, wm=wmt, ffn=ffn, bias3=bias3, bmv=bmv, rwt=rwt,
                  rbv=rbv, tempv=tempv, identv=identv, maskv=maskv)
    in_maps = []
    nb = n_batch_per_core
    for c in range(n_cores):
        xc = x[c * nb:(c + 1) * nb]
        xT = np.ascontiguousarray(xc.transpose(0, 2, 1)).reshape(nb, DCH, 128, S)
        in_maps.append(dict(shared, xT=xT))
    return in_maps


def kernel(**inputs):
    nc = build_program()
    in_maps = prepare_inputs(inputs)
    res = run_bass_kernel_spmd(nc, in_maps, core_ids=list(range(8)))
    outs = []
    for c in range(8):
        yc = res.results[c]["y"]
        yc = yc.reshape(NB, D, S).transpose(0, 2, 1)
        outs.append(yc)
    return np.ascontiguousarray(np.concatenate(outs, axis=0), dtype=np.float32)
